# version 07
# baseline (speedup 1.0000x reference)
import math
import numpy as np
import ml_dtypes
import concourse.bass as bass
import concourse.mybir as mybir
from concourse.bass_utils import run_bass_kernel_spmd

F32 = mybir.dt.float32
BF16 = mybir.dt.bfloat16
I32 = mybir.dt.int32
U8 = mybir.dt.uint8
AF = mybir.ActivationFunctionType
ALU = mybir.AluOpType
AX = mybir.AxisListType

B, N, DM = 2, 16384, 2048
H, HD, CHK = 16, 128, 128
NSEG = 4
OWN = N // NSEG
TILE = 256
GSL = TILE // 128
NT_OWN = OWN // TILE
MLP = 16384
FTOK = 512
NPASS = OWN // FTOK
NGR = 512
ROUNDS = [list(range(0, 6)), list(range(6, 11)), list(range(11, 16))]
NNC = MLP // 128
LOG_LAM = [-(2.0 ** (-(h + 1) / 2.0)) for h in range(H)]
LAM_C = [math.exp(l * CHK) for l in LOG_LAM]
EPS_LN = 1e-5
EPS_G = 1e-6
RBASE = 10000.0
ENGS = ("sync", "scalar", "vector", "tensor", "gpsimd")
KB = 1024
CFG = {"prefix": True, "nown": NT_OWN, "npass": NPASS, "dq": True, "astop": None, "fstop": None, "dump": False, "dqsafe": False}


class Sch:
    def __init__(self, nc):
        self.nc = nc
        self.ops = {e: [] for e in ENGS}
        self.sems = {}
        self.cnt = {}
        self._sect = None

    def sem(self, name):
        if name not in self.sems:
            self.sems[name] = self.nc.alloc_semaphore(name)
            self.cnt[name] = 0
        return self.sems[name]

    def emit(self, eng, fn, waits=(), inc=None, n=1):
        w = tuple(t for t in waits if t is not None)
        tok = None
        if inc is not None:
            self.sem(inc)
            self.cnt[inc] += n
            tok = (inc, self.cnt[inc])
            if self._sect is not None:
                d = self._sect["incs"].setdefault(eng, {})
                d[inc] = d.get(inc, 0) + n
        self.ops[eng].append(("OP", fn, w, inc, n))
        return tok

    def begin_if(self, thresh):
        assert self._sect is None
        self._sect = {"thresh": thresh, "incs": {}}
        for e in ENGS:
            self.ops[e].append(("IF", thresh))

    def end_if(self):
        s = self._sect
        self._sect = None
        for e in ENGS:
            self.ops[e].append(("ELSE", s["incs"].get(e, {})))

    def now(self, name):
        return (name, self.cnt.get(name, 0))

    def replay(self, block, seg_info_view, qs_first):
        sch = self

        def make_body(ename):
            def body(eng):
                last = {}
                stack = []
                guard = [None]
                if ename != "sync":
                    eng.wait_ge(sch.sems["qs"], qs_first)
                    last["qs"] = qs_first
                reg = [None]

                def get_reg():
                    if reg[0] is None:
                        if last.get("qs", 0) < qs_first:
                            eng.wait_ge(sch.sems["qs"], qs_first)
                            last["qs"] = qs_first
                        v = eng.value_load(seg_info_view)
                        reg[0] = eng.to_reg(v)
                    return reg[0]

                for item in sch.ops[ename]:
                    kind = item[0]
                    if kind == "IF":
                        stack.append(dict(last))
                        guard[0] = eng.If_cmp(get_reg(), item[1], "IS_GT")
                        guard[0].__enter__()
                    elif kind == "ELSE":
                        guard[0].__exit__(None, None, None)
                        eg = eng.Else()
                        eg.__enter__()
                        for s, d in item[1].items():
                            eng.sem_inc(sch.sems[s], d)
                        eg.__exit__(None, None, None)
                        last = stack.pop()
                        guard[0] = None
                    else:
                        _, fn, waits, inc, n = item
                        for s, v in waits:
                            if last.get(s, 0) < v:
                                eng.wait_ge(sch.sems[s], v)
                                last[s] = v
                        inst = fn(eng)
                        if inc is not None:
                            if inst is None:
                                eng.sem_inc(sch.sems[inc], n)
                            else:
                                inst.then_inc(sch.sems[inc], n)
            return body

        block.sync(make_body("sync"))
        block.scalar(make_body("scalar"))
        block.vector(make_body("vector"))
        block.tensor(make_body("tensor"))
        block.gpsimd(make_body("gpsimd"))


def host_constants():
    j = np.arange(64, dtype=np.float64)
    theta = RBASE ** (-2.0 * j / 128.0)
    pos = np.arange(N, dtype=np.float64)
    ang = pos[:, None] * theta[None, :]
    c64 = np.cos(ang).T
    s64 = np.sin(ang).T
    cos_fm = np.vstack([c64, c64]).astype(ml_dtypes.bfloat16)
    sin_fm = np.vstack([-s64, s64]).astype(ml_dtypes.bfloat16)

    i = np.arange(CHK, dtype=np.float64)
    diff = i[:, None] - i[None, :]
    DT = np.zeros((H, CHK, CHK), np.float64)
    dqb = np.zeros((H, CHK, CHK), np.float64)
    dk = np.zeros((CHK, H), np.float64)
    for h in range(H):
        DT[h] = np.where(diff <= 0, np.exp(-LOG_LAM[h] * diff), 0.0)
        dqb[h][:, :] = np.exp(LOG_LAM[h] * (i + 1.0))[None, :]
        dk[:, h] = np.exp(LOG_LAM[h] * (CHK - 1.0 - i))
    ind = np.zeros((128, H, H), np.float32)
    ebc = np.zeros((128, H, 128), np.float32)
    for h in range(H):
        ind[:, h, h] = 1.0
        ebc[h, h, :] = 1.0
    CVALS = [-7.5, -2.0, -6.0, EPS_LN, EPS_G, 0.5, 1.0 / HD, 1.0 / DM, 1.0 / 16.0]
    cst = np.zeros((128, 16), np.float32)
    for k, v in enumerate(CVALS):
        cst[:, k] = v
    im = np.zeros((128, 2), np.int32)
    im[:, 0] = 15
    im[:, 1] = 7
    return dict(
        cos_fm=cos_fm, sin_fm=sin_fm,
        DT=np.ascontiguousarray(DT.astype(ml_dtypes.bfloat16).reshape(H * CHK, CHK)),
        dqb=np.ascontiguousarray(dqb.astype(ml_dtypes.bfloat16).reshape(H * CHK, CHK)),
        dk=dk.astype(np.float32),
        idb=np.eye(128, dtype=ml_dtypes.bfloat16),
        idf=np.eye(128, dtype=np.float32),
        ind=np.ascontiguousarray(ind.reshape(128, H * H).astype(ml_dtypes.bfloat16)),
        ebc=np.ascontiguousarray(ebc.reshape(128, H * 128).astype(ml_dtypes.bfloat16)),
        cst=cst, im=im, CVALS=CVALS)


def build_program():
    nc = bass.Bass("TRN2", target_bir_lowering=False, debug=False)
    CON = host_constants()

    def din(name, shape, dtype):
        return nc.dram_tensor(name, list(shape), dtype, kind="ExternalInput").ap()

    x_own = din("x_own", [OWN, DM], F32)
    x_pre = din("x_pre", [(NSEG - 1) * OWN, DM], F32)
    w_in = {p: din("w_" + p, [DM, DM], F32) for p in ("q", "k", "v", "o")}
    wg1_in = din("wg1", [DM, HD], F32)
    wg2_in = din("wg2", [HD, DM], F32)
    nw_in = din("nw_pc", [128, H], F32)
    u8_in = {"g": din("g_u8", [MLP, DM // 2], U8),
             "u": din("u_u8", [MLP, DM // 2], U8),
             "d": din("d_u8", [DM, MLP // 2], U8)}
    sc_in = {"g": din("g_sc", [MLP, DM // 32], F32),
             "u": din("u_sc", [MLP, DM // 32], F32),
             "d": din("d_sc", [DM, MLP // 32], F32)}
    cos_in = din("cos_fm", [128, N], BF16)
    sin_in = din("sin_fm", [128, N], BF16)
    cso_in = din("cso", [128, OWN], BF16)
    sso_in = din("sso", [128, OWN], BF16)
    dt_in = din("c_dt", [H * CHK, CHK], BF16)
    dqb_in = din("c_dqb", [H * CHK, CHK], BF16)
    dk_in = din("c_dk", [CHK, H], F32)
    idb_in = din("c_idb", [128, 128], BF16)
    idf_in = din("c_idf", [128, 128], F32)
    ind_in = din("c_ind", [128, H * H], BF16)
    ebc_in = din("c_ebc", [128, H * 128], BF16)
    cst_in = din("c_cst", [128, 16], F32)
    im_in = din("c_im", [128, 2], I32)
    info_in = din("coreinfo", [128, 2], I32)

    out_own = nc.dram_tensor("out_own", [OWN, DM], F32, kind="ExternalOutput").ap()
    scr_xp = nc.dram_tensor("scr_xp", [OWN, DM], F32, kind="Internal").ap()
    scr_xh2 = nc.dram_tensor("scr_xh2", [OWN, DM], BF16, kind="Internal").ap()
    scr_w = {p: nc.dram_tensor("scr_w" + p, [DM, DM], BF16, kind="Internal").ap()
             for p in ("q", "k", "v", "o")}
    scr_fw = {"g": nc.dram_tensor("scr_fg", [MLP, DM], BF16, kind="Internal").ap(),
              "u": nc.dram_tensor("scr_fu", [MLP, DM], BF16, kind="Internal").ap(),
              "d": nc.dram_tensor("scr_fd", [DM, MLP], BF16, kind="Internal").ap()}
    scr_int = nc.dram_tensor("scr_int", [MLP, FTOK], BF16, kind="Internal").ap()
    scrT = {"g": nc.dram_tensor("scrT_g", [DM, MLP], BF16, kind="Internal").ap(),
            "u": nc.dram_tensor("scrT_u", [DM, MLP], BF16, kind="Internal").ap(),
            "d": nc.dram_tensor("scrT_d", [MLP, DM], BF16, kind="Internal").ap()}

    SBT = 200 * KB
    SB = nc.alloc_sbuf_tensor("arena", [128, SBT], U8)

    def V(off, nelem, dtype, parts=128):
        size = {F32: 4, BF16: 2, I32: 4, U8: 1}[dtype]
        ap = SB[:, off:off + nelem * size].bitcast(dtype)
        if parts != 128:
            ap = ap[0:parts, :]
        return ap

    OFF = {}
    top = [0]

    def region(name, nbytes):
        off = (top[0] + 63) // 64 * 64
        OFF[name] = off
        top[0] = off + nbytes
        assert top[0] <= SBT, f"SBUF overflow at {name}: {top[0]}"
        return off

    region("const", 16 * KB)
    region("xroll", 16 * KB)
    region("xht", 8 * KB)
    region("xhtT", 8 * KB)
    region("qf", 8 * KB)
    region("kf", 8 * KB)
    region("ktok", 8 * KB)
    region("vtok", 8 * KB)
    region("sig", 8 * KB)
    region("on", 8 * KB)
    region("ogb", 8 * KB)
    region("s", 8 * KB)
    region("sb", 4 * KB)
    region("dt", 4 * KB)
    region("dqb", 4 * KB)
    region("cs", 4 * KB)
    region("wa", 16 * KB)
    region("scr", 12 * KB)
    region("stat", 2 * KB)
    region("dq", 23 * KB)
    region("dqout", 8 * KB)
    assert OFF["xhtT"] == OFF["xht"] + 8 * KB

    FO = {}
    FO["xf2"] = OFF["qf"]
    FO["ytok"] = OFF["ktok"]
    FO["gw"] = OFF["xroll"]
    FO["uw"] = OFF["ogb"]
    FO["dw"] = OFF["wa"]
    FO["gel"] = OFF["wa"] + 8 * KB
    FO["up"] = OFF["wa"] + 10 * KB
    FO["inc"] = OFF["wa"] + 12 * KB
    FO["yfm2"] = OFF["scr"]
    FO["xps"] = OFF["dq"]
    FO["intr"] = OFF["dqout"]
    FO["sqd"] = OFF["dqout"] + 4 * KB
    assert OFF["kf"] == OFF["qf"] + 8 * KB and OFF["ktok"] == OFF["kf"] + 8 * KB \
        and OFF["vtok"] == OFF["ktok"] + 8 * KB
    assert OFF["on"] == OFF["sig"] + 8 * KB and OFF["sig"] == OFF["vtok"] + 8 * KB
    assert OFF["xht"] == OFF["xroll"] + 16 * KB
    assert OFF["s"] == OFF["ogb"] + 8 * KB and OFF["cs"] == OFF["dqb"] + 4 * KB

    CO = {}
    cc = [OFF["const"]]

    def cdef(name, nbytes):
        o = (cc[0] + 63) // 64 * 64
        CO[name] = o
        cc[0] = o + nbytes
        assert cc[0] <= OFF["const"] + 16 * KB
    cdef("idb", 256); cdef("idf", 512)
    cdef("ind", H * H * 2); cdef("ebc", H * 128 * 2)
    cdef("dk", H * 4); cdef("nw", H * 4)
    cdef("cst", 64); cdef("im", 8)
    cdef("wg1", H * 128 * 2); cdef("wg2", H * 128 * 2)
    cdef("info", 8)

    def cval(name, nelem, dtype, parts=128):
        return V(CO[name], nelem, dtype, parts)

    for k, v in enumerate(CON["CVALS"]):
        nc.const_aps.register(F32, float(v), cval("cst", 16, F32)[:, k:k + 1])

    sch = Sch(nc)
    E = sch.emit

    def DMA(out_ap, in_ap, **kw):
        return lambda e: e.dma_start(out=out_ap, in_=in_ap, **kw)

    PB = [nc.alloc_psum_tensor(f"pb{i}", [128, 512], F32) for i in range(8)]

    def pbf(i, w=512):
        return PB[i][:, 0:w]

    def pbb(i, w=1024):
        return PB[i][:].bitcast(BF16)[:, 0:w]

    XROLL = lambda par: V(OFF["xroll"] + par * 8 * KB, 2048, F32)
    XHT = V(OFF["xht"], 2 * 2048, BF16)
    XHTT = V(OFF["xhtT"], H * TILE, BF16)
    QF = V(OFF["qf"], H * TILE, BF16)
    KF = V(OFF["kf"], H * TILE, BF16)
    KTOK = V(OFF["ktok"], 2 * 2048, BF16)
    VTOK = V(OFF["vtok"], 2 * 2048, BF16)
    SIG = V(OFF["sig"], H * TILE, BF16)
    ON = V(OFF["on"], H * TILE, BF16)
    OGB = V(OFF["ogb"], H * TILE, BF16)
    SFULL = V(OFF["s"], H * 128, F32)
    SBF = V(OFF["sb"], H * 128, BF16)
    DTT = V(OFF["dt"], H * 128, BF16)
    DQBC = V(OFF["dqb"], H * 128, BF16)
    ROT = V(OFF["xhtT"], H * TILE, BF16)
    XPP = V(OFF["xht"], 2 * 2048, F32)
    XPSL = lambda par: V(OFF["qf"] + par * 4 * KB, 1024, F32)
    XPSL = lambda par: V(OFF["qf"] + par * 8 * KB, 2048, F32)
    XH2 = lambda par: V(OFF["ogb"] + par * 4 * KB, 2048, BF16)

    def cs_view(which, parity):
        return V(OFF["cs"] + (which * 2 + parity) * 512, TILE, BF16)

    def wa_view(slot):
        return V(OFF["wa"] + slot * 4 * KB, 16 * 128, BF16).rearrange(
            "p (a c) -> p a c", c=128)

    XFM = lambda par: V(OFF["scr"] + par * KB, TILE, F32)
    G1B = V(OFF["scr"] + 2 * KB, TILE, BF16)
    QDEC = lambda par: V(OFF["scr"] + 2 * KB + 512 + par * 256, 128, BF16)
    OG2 = lambda par: V(OFF["scr"] + 3 * KB + par * 256, 128, BF16)
    KDEC = lambda par: V(OFF["scr"] + 3 * KB + 512 + par * 256, 128, BF16)
    SCM = lambda par: V(OFF["scr"] + 4 * KB + par * 256, 128, BF16)
    RMSF = V(OFF["scr"] + 4 * KB + 512, TILE, F32, parts=16)
    RMSB = V(OFF["scr"] + 5 * KB + 512, TILE, BF16, parts=16)
    SQD = V(OFF["scr"] + 6 * KB, 2048, BF16)

    def stat_col(col, parity):
        o = OFF["stat"] if True else 0
        return V(o, 512, F32)[:, parity * 16 + col: parity * 16 + col + 1]

    ID_B = cval("idb", 128, BF16)
    ID_F = cval("idf", 128, F32)

    ln_war = {}

    def ln_norm(dst_ap, src_ap, parity, eps, t_src, extra=(), sqd=None, statf=None):
        sq = sqd if sqd is not None else SQD
        sc = statf if statf is not None else stat_col
        su = sc(0, parity); m = sc(1, parity); ssq = sc(2, parity)
        var = sc(3, parity); rstd = sc(4, parity); mm_ = sc(5, parity)
        war = ln_war.get(parity)
        t1 = E("vector", lambda e: e.tensor_reduce(su, src_ap, axis=AX.X, op=ALU.add),
               waits=(t_src, war) + tuple(extra), inc="tv")
        t2 = E("scalar", lambda e: e.activation(sq, src_ap, AF.Square, accum_out=ssq),
               waits=(t_src, war, t1) + tuple(extra), inc="ts")
        t3 = E("vector", lambda e: e.tensor_scalar(m, su, 1.0 / DM, None, ALU.mult),
               waits=(t1,), inc="tv")
        t4 = E("vector", lambda e: e.tensor_scalar(var, ssq, 1.0 / DM, None, ALU.mult),
               waits=(t2,), inc="tv")
        t5 = E("vector", lambda e: e.tensor_mul(mm_, m, m), waits=(t3,), inc="tv")
        t6 = E("vector", lambda e: e.tensor_sub(var, var, mm_), waits=(t4, t5), inc="tv")
        t7 = E("scalar", lambda e: e.activation(rstd, var, AF.Sqrt, bias=eps),
               waits=(t6,), inc="ts")
        t8 = E("vector", lambda e: e.reciprocal(rstd, rstd), waits=(t7,), inc="tv")
        t9 = E("vector", lambda e: e.tensor_scalar(dst_ap, src_ap, m, rstd,
                                                   ALU.subtract, ALU.mult),
               waits=(t8,), inc="tv")
        ln_war[parity] = t9
        return t9

    pb_war = [None] * 8

    def transp(src_fn, dst3, nblk, bank, dtype, waits, evac_eng="scalar"):
        ident = ID_B if dtype == BF16 else ID_F
        pb = pbb(bank, nblk * 128) if dtype == BF16 else pbf(bank, nblk * 128)
        tok = None
        for i in range(nblk):
            w = tuple(waits) + ((pb_war[bank],) if pb_war[bank] else ()) if i == 0 else ()
            tok = E("tensor",
                    (lambda ap_in, ap_out: lambda e: e.transpose(ap_out, ap_in, ident))(
                        src_fn(i), pb[:, i * 128:(i + 1) * 128]),
                    waits=w, inc="tt" if i == nblk - 1 else None)
        src3 = pb.rearrange("p (a c) -> p a c", c=128)
        if evac_eng == "scalar":
            t_ev = E("scalar", lambda e: e.activation(dst3, src3, AF.Copy),
                     waits=(tok,), inc="ts")
        else:
            t_ev = E("vector", lambda e: e.tensor_copy(dst3, src3),
                     waits=(tok,), inc="tv")
        pb_war[bank] = t_ev
        return t_ev

    E("sync", DMA(cval("info", 2, I32), info_in[:]), inc="qs", n=16)
    QS_FIRST = 16
    for outv, inv in [
        (cval("idb", 128, BF16), idb_in[:]),
        (cval("idf", 128, F32), idf_in[:]),
        (cval("ind", H * H, BF16), ind_in[:]),
        (cval("dk", H, F32), dk_in[0:128, :]),
        (cval("nw", H, F32), nw_in[:]),
        (cval("cst", 16, F32), cst_in[:]),
        (cval("im", 2, I32), im_in[:]),
        (V(OFF["dt"], H * 128, BF16).rearrange("p (h c) -> p h c", h=H),
         dt_in.rearrange("(h p) c -> p h c", p=128)),
        (V(OFF["dqb"], H * 128, BF16).rearrange("p (h c) -> p h c", h=H),
         dqb_in.rearrange("(h p) c -> p h c", p=128)),

    ]:
        E("sync", DMA(outv, inv), inc="qs", n=16)
    E("sync", DMA(cval("ebc", H * 128, BF16), ebc_in[:]), inc="qs", n=16)
    t_const = sch.now("qs")
    t_wg1 = E("gpsimd", DMA(cval("wg1", H * 128, BF16).rearrange("p (a c) -> p a c", a=H),
                            wg1_in.rearrange("(a p) c -> p a c", p=128)), inc="qg", n=16)
    t_wg2 = E("gpsimd", DMA(cval("wg2", H * 128, BF16), wg2_in[:]), inc="qg", n=16)

    wc0 = V(OFF["dq"], 2048, BF16)
    wc1 = V(OFF["dq"] + 4 * KB, 2048, BF16)
    t_wcast = {}
    prev = [None, None]
    for p in ("k", "v", "q", "o"):
        lastst = None
        for hc in range(16):
            bv = wc0 if hc % 2 == 0 else wc1
            war = prev[hc % 2]
            tl = E("gpsimd", DMA(bv, w_in[p][hc * 128:(hc + 1) * 128, :]),
                   waits=(war,), inc="qg", n=16)
            ts_ = E("gpsimd", DMA(scr_w[p][hc * 128:(hc + 1) * 128, :], bv),
                    waits=(tl,), inc="qg", n=16)
            prev[hc % 2] = ts_
            lastst = ts_
        t_wcast[p] = lastst

    dqo = OFF["dq"]
    DQ_BY = lambda par: V(dqo + par * KB, 1024, U8)
    DQ_I1 = V(dqo + 2 * KB, 1024, U8)
    DQ_I2 = V(dqo + 3 * KB, 1024, U8)
    DQ_CF = V(dqo + 4 * KB, 1024, F32)
    DQ_SGN = V(dqo + 8 * KB, 1024, F32)
    DQ_JF = V(dqo + 12 * KB, 1024, F32)
    DQ_R1 = V(dqo + 16 * KB, 1024, BF16)
    DQ_R2 = V(dqo + 18 * KB, 1024, BF16)
    DQ_SC = lambda par: V(dqo + 20 * KB + par * 512, 128, F32)
    DQ_OUT = lambda par: V(OFF["dqout"] + par * 4 * KB, 2048, BF16)
    dqs = {"war": None, "outwar": [None, None], "ldr": [t_wcast["o"], t_wcast["o"]],
           "scwar": [None, None], "pend": None}
    dq_done_tokens = []

    def dq_core(t_b, t_s, par, dst_ap, bpar):
        nco = 1024
        im = cval("im", 2, I32).bitcast(U8)
        BY = DQ_BY(bpar)
        t_lo = E("vector", lambda e: e.tensor_scalar(DQ_I1, BY, im[:, 0:1], None,
                                                     ALU.bitwise_and),
                 waits=(t_b, t_const, dqs["war"]), inc="tv")
        t_hi = E("vector", lambda e: e.tensor_scalar(DQ_I2, BY, 4, None,
                                                     ALU.logical_shift_right),
                 waits=(t_b, dqs["war"]), inc="tv")
        outs = []
        for plane, t_pl in ((0, t_lo), (1, t_hi)):
            PLI = DQ_I1 if plane == 0 else DQ_I2
            t_cf = E("vector", (lambda PLI: lambda e: e.tensor_copy(DQ_CF, PLI))(PLI),
                     waits=(t_pl, dqs["war"]), inc="tv")
            t_sg = E("scalar", lambda e: e.activation(DQ_SGN, DQ_CF, AF.Sign, bias=-7.5),
                     waits=(t_cf,), inc="ts")
            t_j1 = E("vector", lambda e: e.scalar_tensor_tensor(DQ_JF, DQ_SGN, -4.0,
                                                                DQ_CF, ALU.mult, ALU.add),
                     waits=(t_sg,), inc="tv")
            t_j2 = E("vector", lambda e: e.tensor_scalar(DQ_JF, DQ_JF, 4.0, None,
                                                         ALU.subtract),
                     waits=(t_j1,), inc="tv")
            t_r1 = E("scalar", lambda e: e.activation(DQ_R1, DQ_JF, AF.Relu,
                                                      scale=0.5, bias=-2.0),
                     waits=(t_j2,), inc="ts")
            t_r2 = E("scalar", lambda e: e.activation(DQ_R2, DQ_JF, AF.Relu, bias=-6.0),
                     waits=(t_j2,), inc="ts")
            t_m1 = E("vector", lambda e: e.scalar_tensor_tensor(DQ_JF, DQ_JF, 0.5,
                                                                DQ_R1, ALU.mult, ALU.add),
                     waits=(t_r1,), inc="tv")
            t_m2 = E("gpsimd", lambda e: e.tensor_tensor(DQ_JF, DQ_JF, DQ_R2, ALU.add),
                     waits=(t_m1, t_r2), inc="tp")
            t_v = E("vector", lambda e: e.scalar_tensor_tensor(DQ_CF, DQ_JF, -1.0,
                                                               DQ_SGN, ALU.mult, ALU.mult),
                    waits=(t_m2,), inc="tv")
            dqs["war"] = t_v
            ov3 = DQ_OUT(par).rearrange("p (a two) -> p two a", two=2)[:, plane, :] \
                             .rearrange("p (a b) -> p a b", b=16)
            cf3 = DQ_CF.rearrange("p (a b) -> p a b", b=16)
            scb = DQ_SC(par)[:, 0:64].unsqueeze(2).broadcast_to([128, 64, 16])
            t_o = E("vector", (lambda ov3, cf3, scb: lambda e: e.tensor_tensor(
                ov3, cf3, scb, ALU.mult))(ov3, cf3, scb),
                waits=(t_v, t_s, dqs["outwar"][par]), inc="tv")
            dqs["scwar"][par] = t_o
            outs.append(t_o)
        t_w = E("sync", DMA(dst_ap, DQ_OUT(par)), waits=tuple(outs), inc="qs", n=16)
        dqs["outwar"][par] = t_w
        dqs["ldr"][bpar] = t_hi
        dq_done_tokens.append(t_w)

    dq_jobs = []
    for mat in ("g", "u"):
        for r0 in range(0, MLP, 128):
            dq_jobs.append((mat, r0, 0))
    for r0 in range(0, DM, 128):
        for cb in range(0, MLP // 2, 1024):
            dq_jobs.append(("d", r0, cb))
    dq_i = [0]

    scr_fence = nc.dram_tensor("scr_fence", [128, 64], U8, kind="Internal").ap()

    def dq_load(j):
        mat, r0, cb = dq_jobs[j]
        bpar = j % 2
        par = j % 2
        if mat == "d":
            src_ap = u8_in["d"][r0:r0 + 128, cb:cb + 1024]
            sc_ap = sc_in["d"][r0:r0 + 128, cb // 16: cb // 16 + 64]
        else:
            src_ap = u8_in[mat][r0:r0 + 128, :]
            sc_ap = sc_in[mat][r0:r0 + 128, :]
        t_b = E("gpsimd", DMA(DQ_BY(bpar), src_ap), waits=(dqs["ldr"][bpar],),
                inc="qg", n=16)
        t_f = E("gpsimd", DMA(scr_fence[:], DQ_BY(bpar)[:, 0:64]), waits=(),
                inc="qg", n=16)
        t_s = E("gpsimd", DMA(DQ_SC(par)[:, 0:64], sc_ap),
                waits=(dqs["scwar"][par],), inc="qg", n=16)
        return (t_f, t_s, par, bpar)

    def dq_drain(k):
        while k > 0 and dq_i[0] < len(dq_jobs):
            j = dq_i[0]
            dq_i[0] += 1
            k -= 1
            mat, r0, cb = dq_jobs[j]
            if dqs["pend"] is None:
                dqs["pend"] = dq_load(j)
            t_f, t_s, par, bpar = dqs["pend"]
            dqs["pend"] = dq_load(dq_i[0]) if dq_i[0] < len(dq_jobs) else None
            if mat == "d":
                dst = scr_fw["d"][r0:r0 + 128, cb * 2: cb * 2 + 2048]
            else:
                dst = scr_fw[mat][r0:r0 + 128, :]
            dq_core(t_f, t_s, par, dst, bpar)
            continue

    st = {
        "xroll": [None, None], "cs": [None, None], "wa": [None] * 4,
        "s": None, "sbf": None, "rotwar": None, "xhtwar": None,
        "qd": [None, None], "kd": [None, None], "og2": [None, None],
        "scm": [None, None], "xfm": [None, None], "xh2": [None, None],
        "xpslwar": [None, None], "sig_war": None, "ogb_war": (),
        "rms_war": None, "xpsl_store": [None, None],
    }
    t0 = E("vector", lambda e: e.memset(SFULL, 0.0), inc="tv")
    st["s"] = t0
    st["sbf"] = E("vector", lambda e: e.memset(SBF, 0.0), waits=(t0,), inc="tv")

    def a_tile(idx, src_x, row0, cosap, sinap, pos0, own):
        par_cs = idx % 2
        t_cc = E("sync", DMA(cs_view(0, par_cs), cosap[:, pos0:pos0 + TILE]),
                 waits=(st["cs"][par_cs],), inc="qs", n=16)
        t_ss = E("sync", DMA(cs_view(1, par_cs), sinap[:, pos0:pos0 + TILE]),
                 waits=(st["cs"][par_cs],), inc="qs", n=16)
        t_xh = []
        for g in range(GSL):
            par = g % 2
            t_x = E("sync", DMA(XROLL(par), src_x[row0 + g * 128: row0 + (g + 1) * 128, :]),
                    waits=(st["xroll"][par],), inc="qs", n=16)
            tk = ln_norm(XHT[:, g * 2048:(g + 1) * 2048], XROLL(par), par, EPS_LN, t_x,
                         extra=(t_const, st["xhtwar"]))
            st["xroll"][par] = tk
            t_xh.append(tk)
        if CFG["astop"] == "ln":
            return
        t_tT = None
        for g in range(GSL):
            for half in range(2):
                blocks = (lambda g, half: lambda i: XHT[:, g * 2048 + (half * 8 + i) * 128:
                                                       g * 2048 + (half * 8 + i) * 128 + 128])(g, half)
                dst3 = XHTT.rearrange("p (a c) -> p a c", c=TILE)[
                    :, half * 8:(half + 1) * 8, g * 128:(g + 1) * 128]
                t_tT = transp(blocks, dst3, 8, 2, BF16, (t_xh[g],))
        if CFG["astop"] == "transp":
            return
        t_vt = None
        t_kf_src = None
        t_qf_src = None
        projs = ["v", "k", "q"] if own else ["v", "k"]
        for p in projs:
            ev_last = None
            for dc in range(16):
                slot = dc % 2 if p != "q" else 2 + dc % 2
                wv = wa_view(slot)
                t_w = E("sync", DMA(wv, scr_w[p].rearrange("(a p) c -> p a c", p=128)[
                    :, :, dc * 128:(dc + 1) * 128]),
                        waits=(st["wa"][slot], t_wcast[p]), inc="qs", n=16)
                bank = dc % 2
                mm_last = None
                for hc in range(16):
                    w = (t_w, t_tT, pb_war[bank]) if hc == 0 else ()
                    mm_last = E("tensor",
                                (lambda wv, hc, bank, s_, p_: lambda e: e.matmul(
                                    pbf(bank, TILE), wv[:, hc, :],
                                    XHTT[:, hc * TILE:(hc + 1) * TILE],
                                    start=s_, stop=p_))(wv, hc, bank, hc == 0, hc == 15),
                                waits=w, inc="tt" if hc == 15 else None)
                st["wa"][slot] = mm_last
                if p == "v":
                    dst = OGB[:, dc * TILE:(dc + 1) * TILE]
                    fn = (lambda d, b: lambda e: e.activation(d, pbf(b, TILE), AF.Copy))(dst, bank)
                elif p == "k":
                    dst = ON[:, dc * TILE:(dc + 1) * TILE]
                    fn = (lambda d, b: lambda e: e.activation(d, pbf(b, TILE), AF.Silu))(dst, bank)
                else:
                    dst = OGB[:, dc * TILE:(dc + 1) * TILE]
                    fn = (lambda d, b: lambda e: e.activation(d, pbf(b, TILE), AF.Silu))(dst, bank)
                wts = (mm_last,)
                if p == "q" and dc == 0:
                    wts = (mm_last, t_vt)
                if p == "v" and dc == 0:
                    wts = (mm_last,) + tuple(st["ogb_war"])
                ev_last = E("scalar", fn, waits=wts, inc="ts")
                pb_war[bank] = ev_last
            if p == "v":
                for dc in range(16):
                    blocks = (lambda dc: lambda i: OGB[:, dc * TILE + i * 128:
                                                      dc * TILE + i * 128 + 128])(dc)
                    dst3 = VTOK.rearrange("p (g c) -> p g c", g=GSL)[:, :, dc * 128:(dc + 1) * 128]
                    t_vt = transp(blocks, dst3, GSL, 3, BF16, (ev_last,), evac_eng="vector")
            elif p == "k":
                t_kf_src = ev_last
            else:
                t_qf_src = ev_last
        if CFG["astop"] == "proj":
            return
        t_sig = None
        if own:
            mm_last = None
            for hc in range(16):
                w = (t_wg1, t_tT, pb_war[4]) if hc == 0 else ()
                mm_last = E("tensor", (lambda hc, s_, p_: lambda e: e.matmul(
                    pbf(4, TILE),
                    cval("wg1", H * 128, BF16).rearrange("p (a c) -> p a c", a=H)[:, hc, :],
                    XHTT[:, hc * TILE:(hc + 1) * TILE], start=s_, stop=p_))(hc, hc == 0, hc == 15),
                    waits=w, inc="tt" if hc == 15 else None)
            t_g1 = E("scalar", lambda e: e.activation(G1B, pbf(4, TILE), AF.Copy),
                     waits=(mm_last,), inc="ts")
            pb_war[4] = t_g1
            last_sig = None
            for dc in range(16):
                bank = 5 + dc % 2
                tkm = E("tensor", (lambda dc, b: lambda e: e.matmul(
                    pbf(b, TILE), cval("wg2", H * 128, BF16)[:, dc * 128:(dc + 1) * 128],
                    G1B, start=True, stop=True))(dc, bank),
                    waits=(t_g1, t_wg2, pb_war[bank]), inc="tt")
                last_sig = E("scalar", (lambda dc, b: lambda e: e.activation(
                    SIG[:, dc * TILE:(dc + 1) * TILE], pbf(b, TILE), AF.Sigmoid))(dc, bank),
                    waits=(tkm, st["sig_war"] if dc == 0 else None), inc="ts")
                pb_war[bank] = last_sig
            t_sig = last_sig

        if CFG["astop"] == "gate":
            return
        def rot(srcb, dstb, t_src):
            tmp3 = ROT.rearrange("p (a c) -> p a c", c=TILE)
            src3 = srcb.rearrange("p (a c) -> p a c", c=TILE)
            dst3 = dstb.rearrange("p (a c) -> p a c", c=TILE)
            cb = cs_view(0, par_cs).unsqueeze(1).broadcast_to([128, H, TILE])
            sb2 = cs_view(1, par_cs).unsqueeze(1).broadcast_to([128, H, TILE])
            w0 = (t_src, t_cc, t_ss, st["rotwar"], t_tT, sch.now("tt"),
                  st["xpsl_store"][0], st["xpsl_store"][1])
            a = E("vector", lambda e: e.tensor_copy(ROT[0:64, :], srcb[64:128, :]),
                  waits=w0, inc="tv")
            b = E("vector", lambda e: e.tensor_copy(ROT[64:128, :], srcb[0:64, :]),
                  waits=(a,), inc="tv")
            c2 = E("vector", lambda e: e.tensor_tensor(tmp3, tmp3, sb2, ALU.mult),
                   waits=(b,), inc="tv")
            d_ = E("vector", lambda e: e.tensor_tensor(dst3, src3, cb, ALU.mult),
                   waits=(c2,), inc="tv")
            f_ = E("vector", lambda e: e.tensor_tensor(dstb[:], dstb[:], ROT[:], ALU.add),
                   waits=(d_,), inc="tv")
            st["rotwar"] = f_
            return f_
        t_kf = rot(ON, KF, t_kf_src)
        t_qf = rot(OGB, QF, t_qf_src) if own else None
        st["xhtwar"] = st["rotwar"]
        st["cs"][par_cs] = st["rotwar"]
        if CFG["astop"] == "rot":
            return
        t_kt = None
        for dc in range(16):
            blocks = (lambda dc: lambda i: KF[:, dc * TILE + i * 128:
                                              dc * TILE + i * 128 + 128])(dc)
            dst3 = KTOK.rearrange("p (g c) -> p g c", g=GSL)[:, :, dc * 128:(dc + 1) * 128]
            t_kt = transp(blocks, dst3, GSL, 3, BF16, (t_kf,), evac_eng="vector")

        if CFG["astop"] == "ktr":
            return
        grms_toks = []
        for c in range(GSL):
            for h in range(H):
                par = (c * H + h) % 2
                t_kd = E("vector", (lambda c, h, par: lambda e: e.tensor_scalar(
                    KDEC(par), KTOK[:, c * 2048 + h * 128: c * 2048 + (h + 1) * 128],
                    cval("dk", H, F32)[:, h:h + 1], None, ALU.mult))(c, h, par),
                    waits=(t_kt, st["kd"][par]), inc="tv")
                if own:
                    bs = 2 + par * 2
                    bo = 3 + par * 2
                    t_qd = E("vector", (lambda c, h, par: lambda e: e.tensor_tensor(
                        QDEC(par), QF[:, h * TILE + c * 128: h * TILE + (c + 1) * 128],
                        DQBC[:, h * 128:(h + 1) * 128], ALU.mult))(c, h, par),
                        waits=(t_qf, st["qd"][par]), inc="tv")
                    t_sc = E("tensor", (lambda c, h, bs: lambda e: e.matmul(
                        pbf(bs, 128), KF[:, h * TILE + c * 128: h * TILE + (c + 1) * 128],
                        QF[:, h * TILE + c * 128: h * TILE + (c + 1) * 128],
                        start=True, stop=True))(c, h, bs),
                        waits=(t_kf, t_qf, pb_war[bs]), inc="tt")
                    t_scm = E("vector", (lambda c, h, par, bs: lambda e: e.tensor_tensor(
                        SCM(par), pbf(bs, 128), DTT[:, h * 128:(h + 1) * 128], ALU.mult))(c, h, par, bs),
                        waits=(t_sc, st["scm"][par]), inc="tv")
                    pb_war[bs] = t_scm
                    E("tensor", (lambda c, h, par, bo: lambda e: e.matmul(
                        pbf(bo, 128), VTOK[:, c * 2048 + h * 128: c * 2048 + (h + 1) * 128],
                        SCM(par), start=True, stop=False))(c, h, par, bo),
                        waits=(t_scm, t_vt, pb_war[bo]), inc=None)
                    t_o2 = E("tensor", (lambda h, par, bo: lambda e: e.matmul(
                        pbf(bo, 128), SBF[:, h * 128:(h + 1) * 128], QDEC(par),
                        start=False, stop=True))(h, par, bo),
                        waits=(t_qd, st["sbf"]), inc="tt")
                    st["qd"][par] = t_o2
                    st["scm"][par] = t_o2
                    t_og = E("vector", (lambda c, h, bo: lambda e: e.tensor_tensor(
                        OGB[:, h * TILE + c * 128: h * TILE + (c + 1) * 128],
                        pbf(bo, 128),
                        SIG[:, h * TILE + c * 128: h * TILE + (c + 1) * 128], ALU.mult))(c, h, bo),
                        waits=(t_o2, t_sig, t_qf), inc="tv")
                    pb_war[bo] = t_og
                    t_og2 = E("scalar", (lambda c, h, par: lambda e: e.activation(
                        OG2(par), OGB[:, h * TILE + c * 128: h * TILE + (c + 1) * 128],
                        AF.Square))(c, h, par),
                        waits=(t_og, st["og2"][par]), inc="ts")
                    t_st = E("tensor", (lambda c, h, par: lambda e: e.matmul(
                        pbf(6, TILE)[0:16, c * 128:(c + 1) * 128],
                        cval("ind", H * H, BF16).rearrange("p (a c) -> p a c", a=H)[:, h, :],
                        OG2(par), start=(h == 0), stop=(h == 15)))(c, h, par),
                        waits=(t_og2,) + ((pb_war[6],) if (h == 0 and c == 0) else ()),
                        inc="tt" if h == 15 else None)
                    st["og2"][par] = t_st if t_st else st["og2"][par]
                    if h == 15:
                        grms_toks.append(t_st)
                bst = (7 if par == 0 else 0) if own else (7 if par == 0 else 6)
                t_su = E("tensor", (lambda c, h, par, bst: lambda e: e.matmul(
                    pbf(bst, 128), KDEC(par),
                    VTOK[:, c * 2048 + h * 128: c * 2048 + (h + 1) * 128],
                    start=True, stop=True))(c, h, par, bst),
                    waits=(t_kd, t_vt, pb_war[bst]), inc="tt")
                st["kd"][par] = t_su
                t_s = E("vector", (lambda h, bst: lambda e: e.scalar_tensor_tensor(
                    SFULL[:, h * 128:(h + 1) * 128], SFULL[:, h * 128:(h + 1) * 128],
                    LAM_C[h], pbf(bst, 128), ALU.mult, ALU.add))(h, bst),
                    waits=(t_su, st["s"]), inc="tv")
                pb_war[bst] = t_s
                st["s"] = t_s
                if own:
                    st["sbf"] = E("vector", (lambda h: lambda e: e.tensor_copy(
                        SBF[:, h * 128:(h + 1) * 128], SFULL[:, h * 128:(h + 1) * 128]))(h),
                        waits=(t_s,), inc="tv")
        if not own:
            return
        if CFG["astop"] == "chunk":
            return

        st["sig_war"] = sch.now("tv")
        t_sq = E("scalar", lambda e: e.activation(RMSF, pbf(6, TILE)[0:16, :], AF.Sqrt,
                                                  scale=1.0 / HD, bias=EPS_G),
                 waits=tuple(grms_toks) + (st["rms_war"],), inc="ts")
        t_rc = E("vector", lambda e: e.reciprocal(RMSF, RMSF), waits=(t_sq,), inc="tv")
        t_rb = E("vector", lambda e: e.tensor_copy(RMSB, RMSF), waits=(t_rc,), inc="tv")
        st["rms_war"] = t_rb
        pb_war[6] = t_rb
        t_on_last = None
        for h in range(H):
            bank = 5 + h % 2
            t_bc = E("tensor", (lambda h, b: lambda e: e.matmul(
                pbf(b, TILE), cval("ebc", H * 128, BF16, parts=16)[:, h * 128:(h + 1) * 128],
                RMSB, start=True, stop=True))(h, bank),
                waits=(t_rb, pb_war[bank]), inc="tt")
            t_t1 = E("vector", (lambda h, b: lambda e: e.tensor_tensor(
                ON[:, h * TILE:(h + 1) * TILE], OGB[:, h * TILE:(h + 1) * TILE],
                pbf(b, TILE), ALU.mult))(h, bank),
                waits=(t_bc,), inc="tv")
            t_on_last = E("vector", (lambda h: lambda e: e.tensor_scalar(
                ON[:, h * TILE:(h + 1) * TILE], ON[:, h * TILE:(h + 1) * TILE],
                cval("nw", H, F32)[:, h:h + 1], None, ALU.mult))(h),
                waits=(t_t1,), inc="tv")
            pb_war[bank] = t_on_last
        if CFG["astop"] == "grms":
            return
        xpp_toks = []
        for dc in range(16):
            slot = dc % 2
            wv = wa_view(slot)
            t_w = E("sync", DMA(wv, scr_w["o"].rearrange("(a p) c -> p a c", p=128)[
                :, :, dc * 128:(dc + 1) * 128]),
                    waits=(st["wa"][slot], t_wcast["o"]), inc="qs", n=16)
            bank = dc % 2
            mm_last = None
            for ec in range(16):
                w = (t_w, t_on_last, pb_war[bank]) if ec == 0 else ()
                mm_last = E("tensor", (lambda wv, ec, bank, s_, p_: lambda e: e.matmul(
                    pbf(bank, TILE), wv[:, ec, :], ON[:, ec * TILE:(ec + 1) * TILE],
                    start=s_, stop=p_))(wv, ec, bank, ec == 0, ec == 15),
                    waits=w, inc="tt" if ec == 15 else None)
            st["wa"][slot] = mm_last
            parf = dc % 2
            t_ev = E("vector", (lambda b, parf: lambda e: e.tensor_copy(
                XFM(parf), pbf(b, TILE)))(bank, parf),
                waits=(mm_last, st["xfm"][parf]), inc="tv")
            pb_war[bank] = t_ev
            blocks = (lambda parf: lambda i: XFM(parf)[:, i * 128:(i + 1) * 128])(parf)
            dst3 = XPP.rearrange("p (g c) -> p g c", g=GSL)[:, :, dc * 128:(dc + 1) * 128]
            tk = transp(blocks, dst3, GSL, 4, F32, (t_ev,), evac_eng="vector")
            st["xfm"][parf] = tk
            xpp_toks.append(tk)
        if CFG["astop"] == "wo":
            return
        t_ln3 = None
        for g in range(GSL):
            par = g % 2
            xsl = XPP[:, g * 2048:(g + 1) * 2048]
            t_x = E("sync", DMA(XROLL(par), src_x[row0 + g * 128: row0 + (g + 1) * 128, :]),
                    waits=(st["xroll"][par],), inc="qs", n=16)
            t_n = ln_norm(XPSL(par), xsl, par, EPS_LN, xpp_toks[-1],
                          extra=tuple(xpp_toks) + (st["xpslwar"][par], sch.now("tt")))
            t_r = E("vector", (lambda par: lambda e: e.tensor_tensor(
                XPSL(par), XPSL(par), XROLL(par), ALU.add))(par),
                waits=(t_n, t_x), inc="tv")
            st["xroll"][par] = t_r
            t_xpst = E("sync", DMA(scr_xp[row0 + g * 128: row0 + (g + 1) * 128, :], XPSL(par)),
                       waits=(t_r,), inc="qs", n=16)
            st["xpsl_store"][par] = t_xpst
            t_ln3 = ln_norm(XH2(par), XPSL(par), par, EPS_LN, t_r,
                            extra=(st["xh2"][par],))
            st["xpslwar"][par] = t_ln3
            t_st2 = E("sync", DMA(scr_xh2[row0 + g * 128: row0 + (g + 1) * 128, :], XH2(par)),
                      waits=(t_ln3,), inc="qs", n=16)
            st["xh2"][par] = t_st2
        st["xhtwar"] = t_ln3
        st["ogb_war"] = (sch.now("tv"), st["xh2"][0], st["xh2"][1])

    for k in range(NSEG - 1 if CFG["prefix"] else 0):
        sch.begin_if(k)
        for ti in range(NT_OWN):
            row0 = k * OWN + ti * TILE
            a_tile(k * NT_OWN + ti, x_pre, row0, cos_in, sin_in, row0, own=False)
        sch.end_if()
    st["sbf"] = E("vector", lambda e: e.tensor_copy(SBF, SFULL),
                  waits=(st["s"], st["sbf"]), inc="tv")
    for ti in range(CFG["nown"]):
        a_tile(100 + ti, x_own, ti * TILE, cso_in, sso_in, ti * TILE, own=True)
        if CFG["dq"]:
            dq_drain(12)

    XF2 = V(FO["xf2"], H * FTOK, BF16)
    GW = lambda par: V(FO["gw"] + par * 16 * KB, H * NGR, BF16)
    UW = lambda par: V(FO["uw"] + par * 16 * KB, H * NGR, BF16)
    DW = lambda par: V(FO["dw"] + par * 4 * KB, 2048, BF16)
    GEL = lambda par: V(FO["gel"] + par * KB, FTOK, BF16)
    UPV = lambda par: V(FO["up"] + par * KB, FTOK, BF16)
    INC = lambda par: V(FO["inc"] + par * KB, FTOK, BF16)
    YFMP = lambda par: V(FO["yfm2"] + par * 2 * KB, FTOK, F32)
    YTOKA = V(FO["ytok"], 4 * 2048, F32)
    YTOK = lambda g: YTOKA[:, g * 2048:(g + 1) * 2048]
    XPS = lambda par: V(FO["xps"] + par * 8 * KB, 2048, F32)
    INTR = lambda par: V(FO["intr"] + par * KB, FTOK, BF16)
    SQD4 = V(FO["sqd"], 2048, BF16)

    fs = {"xf2": None, "gw": [None, None], "uw": [None, None], "dw": [None, None],
          "gel": [None, None], "up": [None, None], "inc": [None, None],
          "intr": [None] * 4, "yfm": [None, None], "ytok": None,
          "xps": [None, None], "outw": [None] * 4, "gwtok": None}
    if CFG["dq"]:
        dq_drain(10 ** 9)
    t_dq_all = dq_done_tokens[-1] if dq_done_tokens else None
    t_a_done = sch.now("qs")
    t_a_bar = (t_a_done, st["s"], st["sbf"], sch.now("tt"), sch.now("tv"), sch.now("ts"))
    tp_war = [None, None, None, None]
    GWB0 = lambda sl: V(FO["gw"] + sl * 8 * KB, 2048, BF16)
    tp_jobs = []
    for mat, src_t, dst_t in (("g", scr_fw["g"], scrT["g"]),
                              ("u", scr_fw["u"], scrT["u"])):
        for hc in range(16):
            for nb in range(8):
                tp_jobs.append((src_t[nb * 2048:(nb + 1) * 2048,
                                      hc * 128:(hc + 1) * 128],
                                dst_t[hc * 128:(hc + 1) * 128,
                                      nb * 2048:(nb + 1) * 2048]))
    for nt in range(128):
        tp_jobs.append((scr_fw["d"][:, nt * 128:(nt + 1) * 128],
                        scrT["d"][nt * 128:(nt + 1) * 128, :]))
    tp_pend = None
    for k, (sap, dap) in enumerate(tp_jobs):
        sl = k % 4
        tl = E("scalar", DMA(GWB0(sl), sap, transpose=True),
               waits=(tp_war[sl] if tp_war[sl] else t_a_bar[0], t_dq_all)
               + (t_a_bar if k < 4 else ()),
               inc="qa", n=16)
        if tp_pend is not None:
            ptl, pdap, psl = tp_pend
            ts2 = E("sync", DMA(pdap, GWB0(psl)), waits=(ptl, tl), inc="qs", n=16)
            tp_war[psl] = ts2
        tp_pend = (tl, dap, sl)
    ptl, pdap, psl = tp_pend
    ts2 = E("sync", DMA(pdap, GWB0(psl)), waits=(ptl, sch.now("qa")), inc="qs", n=16)
    tp_war[psl] = ts2
    E("sync", DMA(scr_fence[:], GWB0(psl)[:, 0:32].bitcast(U8)), waits=(), inc="qs", n=16)
    t_tp_all = sch.now("qs")

    NGRP = MLP // NGR
    for p in range(CFG["npass"]):
        tok0 = p * FTOK

        def gw_load(grp):
            gpar = grp % 2
            for h4 in range(4):
                E("sync", DMA(GW(gpar)[:, h4 * 4 * NGR:(h4 + 1) * 4 * NGR]
                              .rearrange("p (a c) -> p a c", a=4),
                              scrT["g"].rearrange("(a p) c -> p a c", p=128)[
                                  :, h4 * 4:(h4 + 1) * 4,
                                  grp * NGR:(grp + 1) * NGR]),
                  waits=(fs["gw"][gpar], t_tp_all if p == 0 else None),
                  inc="qs", n=16)
                E("sync", DMA(UW(gpar)[:, h4 * 4 * NGR:(h4 + 1) * 4 * NGR]
                              .rearrange("p (a c) -> p a c", a=4),
                              scrT["u"].rearrange("(a p) c -> p a c", p=128)[
                                  :, h4 * 4:(h4 + 1) * 4,
                                  grp * NGR:(grp + 1) * NGR]),
                  waits=(fs["uw"][gpar],), inc="qs", n=16)
            return E("sync", DMA(scr_fence[:], GW(gpar)[:, 0:32].bitcast(U8)),
                     waits=(), inc="qs", n=16)

        fs[("gwt", 0)] = gw_load(0)
        for hc in range(16):
            E("sync", DMA(XF2[:, hc * FTOK:(hc + 1) * FTOK],
                          scr_xh2[tok0:tok0 + FTOK, hc * 128:(hc + 1) * 128],
                          transpose=True),
              waits=(fs["xf2"],) + (t_a_bar if (p == 0 and hc == 0) else ()),
              inc="qs", n=16)
        t_xf2 = E("sync", DMA(scr_fence[:], XF2[:, 0:32].bitcast(U8)),
                  waits=(), inc="qs", n=16)
        ytok_cols = {}
        if CFG["fstop"] == "xf2":
            fs["xf2"] = sch.now("qs")
            continue
        rounds_eff = ROUNDS[:1] if CFG["fstop"] in ("gate", "r0") else ROUNDS
        for r, hcs in enumerate(rounds_eff):
            nbank = {hc: i for i, hc in enumerate(hcs)}
            last_dw_mm = None
            pend = None
            intr_next = [0]
            intr_tok = {}

            def intr_prefetch(upto):
                while intr_next[0] <= min(upto, NNC - 1):
                    i2 = intr_next[0]
                    ip = i2 % 4
                    intr_tok[i2] = E("sync", DMA(INTR(ip), scr_int[i2 * 128:i2 * 128 + 128, :]),
                                     waits=(fs["intr"][ip],
                                            fs[("ints", p)] if i2 < 4 else None)
                                     + ((t_a_bar + (t_dq_all,)) if (p == 0 and r == 1 and i2 < 4) else ()),
                                     inc="qs", n=16)
                    intr_next[0] += 1

            def emit_down(pd, ncix_d, fence=None):
                t_ird, int_ap, t_dw, dpar = pd
                mm_last = None
                for i, hc in enumerate(hcs):
                    bank = nbank[hc]
                    w = [t_dw, t_ird, fence]
                    if ncix_d == 0:
                        w.append(pb_war[bank])
                    mm_last = E("tensor", (lambda i, bank, dpar, int_ap, s_, p_: lambda e: e.matmul(
                        pbf(bank), DW(dpar)[:, i * 128:(i + 1) * 128], int_ap,
                        start=s_, stop=p_))(i, bank, dpar, int_ap,
                                            ncix_d == 0, ncix_d == NNC - 1),
                        waits=tuple(w), inc="tt" if i == len(hcs) - 1 else None)
                fs["dw"][dpar] = mm_last
                if r == 0:
                    fs[("incrd", ncix_d % 2)] = mm_last
                else:
                    fs["intr"][ncix_d % 4] = mm_last
                return mm_last

            for ncix in range(NNC):
                nc0 = ncix * 128
                if r == 0:
                    grp = ncix // (NGR // 128)
                    within = ncix % (NGR // 128)
                    gpar = grp % 2
                    if within == 0 and grp + 1 < NGRP:
                        fs[("gwt", grp + 1)] = gw_load(grp + 1)
                    t_w = fs[("gwt", grp)]
                    gp2 = ncix % 2
                    mm_last = None
                    for hc in range(16):
                        w = (t_w, t_xf2, pb_war[6]) if hc == 0 else ()
                        mm_last = E("tensor", (lambda hc, gpar, within, s_, p_: lambda e: e.matmul(
                            pbf(6), GW(gpar)[:, hc * NGR + within * 128:
                                             hc * NGR + within * 128 + 128],
                            XF2[:, hc * FTOK:(hc + 1) * FTOK], start=s_, stop=p_))(
                            hc, gpar, within, hc == 0, hc == 15),
                            waits=w, inc="tt" if hc == 15 else None)
                    t_gel = E("scalar", (lambda gp2: lambda e: e.activation(
                        GEL(gp2), pbf(6), AF.Gelu_apprx_tanh))(gp2),
                        waits=(mm_last, fs["gel"][gp2])
                        + (t_a_bar if (p == 0 and ncix < 2) else ()), inc="ts")
                    pb_war[6] = t_gel
                    mm_last = None
                    for hc in range(16):
                        w = (t_w, t_xf2, pb_war[7]) if hc == 0 else ()
                        mm_last = E("tensor", (lambda hc, gpar, within, s_, p_: lambda e: e.matmul(
                            pbf(7), UW(gpar)[:, hc * NGR + within * 128:
                                             hc * NGR + within * 128 + 128],
                            XF2[:, hc * FTOK:(hc + 1) * FTOK], start=s_, stop=p_))(
                            hc, gpar, within, hc == 0, hc == 15),
                            waits=w, inc="tt" if hc == 15 else None)
                    if within == (NGR // 128) - 1:
                        fs["gw"][gpar] = mm_last
                        fs["uw"][gpar] = mm_last
                    t_up = E("scalar", (lambda gp2: lambda e: e.activation(
                        UPV(gp2), pbf(7), AF.Copy))(gp2),
                        waits=(mm_last, fs["up"][gp2]), inc="ts")
                    pb_war[7] = t_up
                    t_int = E("vector", (lambda gp2: lambda e: e.tensor_tensor(
                        INC(gp2), GEL(gp2), UPV(gp2), ALU.mult))(gp2),
                        waits=(t_gel, t_up, fs["inc"][gp2],
                               fs.get(("incrd", gp2))), inc="tv")
                    fs["gel"][gp2] = t_int
                    fs["up"][gp2] = t_int
                    t_is = E("sync", DMA(scr_int[nc0:nc0 + 128, :], INC(gp2)),
                             waits=(t_int,), inc="qs", n=16)
                    fs["inc"][gp2] = t_is
                    t_ird = t_int
                    int_ap = INC(gp2)
                    if ncix == NNC - 1:
                        fs[("ints", p)] = t_is
                else:
                    if CFG["fstop"] == "gate":
                        continue
                    intr_prefetch(ncix)
                    t_ird = intr_tok[ncix]
                    int_ap = INTR(ncix % 4)
                if CFG["fstop"] == "gate":
                    continue
                dpar = ncix % 2
                ncols = len(hcs) * 128
                t_dw = E("sync", DMA(DW(dpar)[:, 0:ncols],
                                     scrT["d"][nc0:nc0 + 128,
                                               hcs[0] * 128: hcs[0] * 128 + ncols]),
                         waits=(fs["dw"][dpar], t_tp_all if (p == 0 and r == 0) else None)
                         + (t_a_bar if (p == 0 and r == 0 and ncix < 2) else ()),
                         inc="qs", n=16)
                if pend is not None:
                    emit_down(pend, ncix - 1, fence=t_dw)
                    if r > 0:
                        intr_prefetch(ncix + 3)
                pend = (t_ird, int_ap, t_dw, dpar)
            if CFG["fstop"] == "gate":
                continue
            t_fdw = E("sync", DMA(scr_fence[:], DW(pend[3])[:, 0:32].bitcast(U8)),
                      waits=(), inc="qs", n=16)
            last_dw_mm = emit_down(pend, NNC - 1, fence=t_fdw)
            if CFG["fstop"] == "gate":
                continue
            for i, hc in enumerate(hcs):
                bank = nbank[hc]
                ypar = i % 2
                t_evy = E("scalar", (lambda bank, ypar: lambda e: e.activation(
                    YFMP(ypar), pbf(bank), AF.Copy))(bank, ypar),
                    waits=(last_dw_mm, fs["yfm"][ypar]), inc="ts")
                pb_war[bank] = t_evy
                tkt = None
                for g in range(4):
                    w = (t_evy,) if g == 0 else ()
                    tkt = E("tensor", (lambda ypar, g, bank: lambda e: e.transpose(
                        pbf(bank)[:, g * 128:(g + 1) * 128],
                        YFMP(ypar)[:, g * 128:(g + 1) * 128], ID_F))(ypar, g, bank),
                        waits=w, inc="tt" if g == 3 else None)
                dst3 = YTOKA.rearrange("p (g c) -> p g c", g=4)[:, :, hc * 128:(hc + 1) * 128]
                wz = (tkt,)
                if r == 0 and i == 0:
                    wz = (tkt, fs["outw"][0], fs["outw"][1], fs["outw"][2],
                          fs["outw"][3])
                    if p == 0:
                        wz = wz + t_a_bar
                t_evt = E("vector", (lambda dst3, bank: lambda e: e.tensor_copy(
                    dst3, pbf(bank).rearrange("p (a c) -> p a c", c=128)))(dst3, bank),
                    waits=wz, inc="tv")
                pb_war[bank] = t_evt
                fs["yfm"][ypar] = t_evt
                ytok_cols[hc] = t_evt
        if CFG["fstop"] in ("xf2", "gate", "r0"):
            fs["xf2"] = sch.now("tt")
            continue
        t_cols = tuple(ytok_cols[hc] for hc in range(H))
        t_xpl = [None] * 5
        for g in range(4):
            par = g % 2
            if g == 0:
                t_xpl[0] = E("sync", DMA(XPS(0), scr_xp[tok0: tok0 + 128, :]),
                             waits=(fs["xps"][0],)
                             + ((t_a_bar + (t_dq_all,)) if p == 0 else ()),
                             inc="qs", n=16)
            if g + 1 < 4:
                t_xpl[g + 1] = E("sync", DMA(XPS((g + 1) % 2),
                                             scr_xp[tok0 + (g + 1) * 128:
                                                    tok0 + (g + 2) * 128, :]),
                                 waits=(fs["xps"][(g + 1) % 2],)
                                 + ((t_a_bar + (t_dq_all,)) if (p == 0 and g == 0) else ()),
                                 inc="qs", n=16)
            else:
                t_xpl[4] = E("sync", DMA(scr_fence[:], XPS(par)[:, 0:16].bitcast(U8)),
                             waits=(), inc="qs", n=16)
            t_n = ln_norm(YTOK(g), YTOK(g), 4 + g % 2, EPS_LN, t_cols[-1],
                          extra=t_cols, sqd=SQD4)
            t_add = E("vector", (lambda g, par: lambda e: e.tensor_tensor(
                YTOK(g), YTOK(g), XPS(par), ALU.add))(g, par),
                waits=(t_n, t_xpl[g], t_xpl[g + 1]), inc="tv")
            fs["xps"][par] = t_add
            t_o = E("sync", DMA(out_own[tok0 + g * 128: tok0 + (g + 1) * 128, :], YTOK(g)),
                    waits=(t_add,), inc="qs", n=16)
            fs["outw"][g] = t_o
        fs["xf2"] = sch.now("tt")
        fs["ytok"] = fs["outw"][3]

    if CFG["dump"]:
        dump_xp = nc.dram_tensor("dump_xp", [OWN, DM], F32, kind="ExternalOutput").ap()
        dump_xh2 = nc.dram_tensor("dump_xh2", [OWN, DM], BF16, kind="ExternalOutput").ap()
        E("gpsimd", DMA(dump_xp[:], scr_xp[:]), waits=(t_a_done,), inc="qg", n=16)
        E("gpsimd", DMA(dump_xh2[:], scr_xh2[:]), waits=(t_a_done,), inc="qg", n=16)
        dump_fg = nc.dram_tensor("dump_fg", [MLP, DM], BF16, kind="ExternalOutput").ap()
        dump_fg2 = nc.dram_tensor("dump_fg2", [MLP, DM], BF16, kind="ExternalOutput").ap()
        dump_fd = nc.dram_tensor("dump_fd", [DM, MLP], BF16, kind="ExternalOutput").ap()
        wdq = (t_dq_all, sch.now("qs"), sch.now("tv"), sch.now("ts"), sch.now("tp"),
               sch.now("qa"), sch.now("qg"))
        t_d1 = E("gpsimd", DMA(dump_fg[:], scr_fw["g"][:]), waits=wdq, inc="qg", n=16)
        t_d2 = E("gpsimd", DMA(dump_fg2[:], scr_fw["g"][:]), waits=(t_d1,), inc="qg", n=16)
        E("gpsimd", DMA(dump_fd[:], scr_fw["d"][:]), waits=wdq, inc="qg", n=16)
        E("gpsimd", lambda e: None, waits=(sch.now("qg"),), inc=None)
    t_last = sch.now("qs")
    E("sync", lambda e: None, waits=(t_last,), inc=None)

    with nc.Block() as block:
        sch.replay(block, cval("info", 2, I32, parts=1)[0:1, 0:1], QS_FIRST)
    return nc


def kernel(L0_attn_x, L0_attn_Wq, L0_attn_Wk, L0_attn_Wv, L0_attn_Wo,
           L0_attn_Wg1, L0_attn_Wg2, L0_attn_norm_weight,
           L0_ffn_gate_W_packed, L0_ffn_gate_scales, L0_ffn_up_W_packed,
           L0_ffn_up_scales, L0_ffn_down_W_packed, L0_ffn_down_scales):
    CON = host_constants()
    x = np.ascontiguousarray(np.asarray(L0_attn_x, dtype=np.float32))

    def u8(a):
        return np.ascontiguousarray(np.asarray(a, np.int32).view(np.uint8)[:, 0::4])

    common = dict(
        w_q=np.asarray(L0_attn_Wq, np.float32), w_k=np.asarray(L0_attn_Wk, np.float32),
        w_v=np.asarray(L0_attn_Wv, np.float32), w_o=np.asarray(L0_attn_Wo, np.float32),
        wg1=np.asarray(L0_attn_Wg1, np.float32), wg2=np.asarray(L0_attn_Wg2, np.float32),
        nw_pc=np.ascontiguousarray(np.asarray(L0_attn_norm_weight, np.float32)
                                   .reshape(H, 128).T),
        g_u8=u8(L0_ffn_gate_W_packed), u_u8=u8(L0_ffn_up_W_packed),
        d_u8=u8(L0_ffn_down_W_packed),
        g_sc=np.asarray(L0_ffn_gate_scales, np.float32),
        u_sc=np.asarray(L0_ffn_up_scales, np.float32),
        d_sc=np.asarray(L0_ffn_down_scales, np.float32),
        cos_fm=CON["cos_fm"], sin_fm=CON["sin_fm"],
        c_dt=CON["DT"], c_dqb=CON["dqb"], c_dk=CON["dk"],
        c_idb=CON["idb"], c_idf=CON["idf"], c_ind=CON["ind"], c_ebc=CON["ebc"],
        c_cst=CON["cst"], c_im=CON["im"])
    in_maps = []
    for c in range(8):
        b, seg = c // 4, c % 4
        m = dict(common)
        m["x_own"] = np.ascontiguousarray(x[b, seg * OWN:(seg + 1) * OWN, :])
        m["x_pre"] = np.ascontiguousarray(x[b, : (NSEG - 1) * OWN, :])
        m["cso"] = np.ascontiguousarray(CON["cos_fm"][:, seg * OWN:(seg + 1) * OWN])
        m["sso"] = np.ascontiguousarray(CON["sin_fm"][:, seg * OWN:(seg + 1) * OWN])
        m["coreinfo"] = np.tile(np.array([[seg, c]], np.int32), (128, 1))
        in_maps.append(m)
    nc = build_program()
    res = run_bass_kernel_spmd(nc, in_maps, core_ids=list(range(8)))
    out = np.zeros((B, N, DM), np.float32)
    for c in range(8):
        b, seg = c // 4, c % 4
        out[b, seg * OWN:(seg + 1) * OWN, :] = res.results[c]["out_own"]
    if CFG["dump"]:
        kernel.dumps = [dict(r) for r in res.results]
    return out


# version 08 (final)
# speedup vs baseline: 1.4981x; 1.4981x over previous
import math
import numpy as np
import ml_dtypes
import concourse.bass as bass
import concourse.mybir as mybir
from concourse.bass_utils import run_bass_kernel_spmd

F32 = mybir.dt.float32
BF16 = mybir.dt.bfloat16
I32 = mybir.dt.int32
U8 = mybir.dt.uint8
AF = mybir.ActivationFunctionType
ALU = mybir.AluOpType
AX = mybir.AxisListType

B, N, DM = 2, 16384, 2048
H, HD, CHK = 16, 128, 128
NSEG = 4
OWN = N // NSEG
TILE = 256
GSL = TILE // 128
NT_OWN = OWN // TILE
MLP = 16384
FTOK = 512
NPASS = OWN // FTOK
NGR = 512
ROUNDS = [list(range(0, 6)), list(range(6, 11)), list(range(11, 16))]
NNC = MLP // 128
LOG_LAM = [-(2.0 ** (-(h + 1) / 2.0)) for h in range(H)]
LAM_C = [math.exp(l * CHK) for l in LOG_LAM]
EPS_LN = 1e-5
EPS_G = 1e-6
RBASE = 10000.0
ENGS = ("sync", "scalar", "vector", "tensor", "gpsimd")
KB = 1024
CFG = {"prefix": True, "nown": NT_OWN, "npass": NPASS, "dq": True, "astop": None, "fstop": None, "dump": False, "dqsafe": False}


class Sch:
    def __init__(self, nc):
        self.nc = nc
        self.ops = {e: [] for e in ENGS}
        self.sems = {}
        self.cnt = {}
        self._sect = None

    def sem(self, name):
        if name not in self.sems:
            self.sems[name] = self.nc.alloc_semaphore(name)
            self.cnt[name] = 0
        return self.sems[name]

    def emit(self, eng, fn, waits=(), inc=None, n=1):
        w = tuple(t for t in waits if t is not None)
        tok = None
        if inc is not None:
            self.sem(inc)
            self.cnt[inc] += n
            tok = (inc, self.cnt[inc])
            if self._sect is not None:
                d = self._sect["incs"].setdefault(eng, {})
                d[inc] = d.get(inc, 0) + n
        self.ops[eng].append(("OP", fn, w, inc, n))
        return tok

    def begin_if(self, thresh):
        assert self._sect is None
        self._sect = {"thresh": thresh, "incs": {}}
        for e in ENGS:
            self.ops[e].append(("IF", thresh))

    def end_if(self):
        s = self._sect
        self._sect = None
        for e in ENGS:
            self.ops[e].append(("ELSE", s["incs"].get(e, {})))

    def now(self, name):
        return (name, self.cnt.get(name, 0))

    def replay(self, block, seg_info_view, qs_first):
        sch = self

        def make_body(ename):
            def body(eng):
                last = {}
                stack = []
                guard = [None]
                if ename != "sync":
                    eng.wait_ge(sch.sems["qs"], qs_first)
                    last["qs"] = qs_first
                reg = [None]

                def get_reg():
                    if reg[0] is None:
                        if last.get("qs", 0) < qs_first:
                            eng.wait_ge(sch.sems["qs"], qs_first)
                            last["qs"] = qs_first
                        v = eng.value_load(seg_info_view)
                        reg[0] = eng.to_reg(v)
                    return reg[0]

                for item in sch.ops[ename]:
                    kind = item[0]
                    if kind == "IF":
                        stack.append(dict(last))
                        guard[0] = eng.If_cmp(get_reg(), item[1], "IS_GT")
                        guard[0].__enter__()
                    elif kind == "ELSE":
                        guard[0].__exit__(None, None, None)
                        eg = eng.Else()
                        eg.__enter__()
                        for s, d in item[1].items():
                            eng.sem_inc(sch.sems[s], d)
                        eg.__exit__(None, None, None)
                        last = stack.pop()
                        guard[0] = None
                    else:
                        _, fn, waits, inc, n = item
                        for s, v in waits:
                            if last.get(s, 0) < v:
                                eng.wait_ge(sch.sems[s], v)
                                last[s] = v
                        inst = fn(eng)
                        if inc is not None:
                            if inst is None:
                                eng.sem_inc(sch.sems[inc], n)
                            else:
                                inst.then_inc(sch.sems[inc], n)
            return body

        block.sync(make_body("sync"))
        block.scalar(make_body("scalar"))
        block.vector(make_body("vector"))
        block.tensor(make_body("tensor"))
        block.gpsimd(make_body("gpsimd"))


def host_constants():
    j = np.arange(64, dtype=np.float64)
    theta = RBASE ** (-2.0 * j / 128.0)
    pos = np.arange(N, dtype=np.float64)
    ang = pos[:, None] * theta[None, :]
    c64 = np.cos(ang).T
    s64 = np.sin(ang).T
    cos_fm = np.vstack([c64, c64]).astype(ml_dtypes.bfloat16)
    sin_fm = np.vstack([-s64, s64]).astype(ml_dtypes.bfloat16)

    i = np.arange(CHK, dtype=np.float64)
    diff = i[:, None] - i[None, :]
    DT = np.zeros((H, CHK, CHK), np.float64)
    dqb = np.zeros((H, CHK, CHK), np.float64)
    dk = np.zeros((CHK, H), np.float64)
    for h in range(H):
        DT[h] = np.where(diff <= 0, np.exp(-LOG_LAM[h] * diff), 0.0)
        dqb[h][:, :] = np.exp(LOG_LAM[h] * (i + 1.0))[None, :]
        dk[:, h] = np.exp(LOG_LAM[h] * (CHK - 1.0 - i))
    ind = np.zeros((128, H, H), np.float32)
    ebc = np.zeros((128, H, 128), np.float32)
    for h in range(H):
        ind[:, h, h] = 1.0
        ebc[h, h, :] = 1.0
    CVALS = [-7.5, -2.0, -6.0, EPS_LN, EPS_G, 0.5, 1.0 / HD, 1.0 / DM, 1.0 / 16.0]
    cst = np.zeros((128, 16), np.float32)
    for k, v in enumerate(CVALS):
        cst[:, k] = v
    im = np.zeros((128, 2), np.int32)
    im[:, 0] = 15
    im[:, 1] = 7
    return dict(
        cos_fm=cos_fm, sin_fm=sin_fm,
        DT=np.ascontiguousarray(DT.astype(ml_dtypes.bfloat16).reshape(H * CHK, CHK)),
        dqb=np.ascontiguousarray(dqb.astype(ml_dtypes.bfloat16).reshape(H * CHK, CHK)),
        dk=dk.astype(np.float32),
        idb=np.eye(128, dtype=ml_dtypes.bfloat16),
        idf=np.eye(128, dtype=np.float32),
        ind=np.ascontiguousarray(ind.reshape(128, H * H).astype(ml_dtypes.bfloat16)),
        ebc=np.ascontiguousarray(ebc.reshape(128, H * 128).astype(ml_dtypes.bfloat16)),
        cst=cst, im=im, CVALS=CVALS)


def build_program():
    nc = bass.Bass("TRN2", target_bir_lowering=False, debug=False)
    CON = host_constants()

    def din(name, shape, dtype):
        return nc.dram_tensor(name, list(shape), dtype, kind="ExternalInput").ap()

    x_own = din("x_own", [OWN, DM], F32)
    x_pre = din("x_pre", [(NSEG - 1) * OWN, DM], F32)
    w_in = {p: din("w_" + p, [DM, DM], F32) for p in ("q", "k", "v", "o")}
    wg1_in = din("wg1", [DM, HD], F32)
    wg2_in = din("wg2", [HD, DM], F32)
    nw_in = din("nw_pc", [128, H], F32)
    u8_in = {"g": din("g_u8", [MLP, DM // 2], U8),
             "u": din("u_u8", [MLP, DM // 2], U8),
             "d": din("d_u8", [DM, MLP // 2], U8)}
    sc_in = {"g": din("g_sc", [MLP, DM // 32], F32),
             "u": din("u_sc", [MLP, DM // 32], F32),
             "d": din("d_sc", [DM, MLP // 32], F32)}
    cos_in = din("cos_fm", [128, N], BF16)
    sin_in = din("sin_fm", [128, N], BF16)
    cso_in = din("cso", [128, OWN], BF16)
    sso_in = din("sso", [128, OWN], BF16)
    dt_in = din("c_dt", [H * CHK, CHK], BF16)
    dqb_in = din("c_dqb", [H * CHK, CHK], BF16)
    dk_in = din("c_dk", [CHK, H], F32)
    idb_in = din("c_idb", [128, 128], BF16)
    idf_in = din("c_idf", [128, 128], F32)
    ind_in = din("c_ind", [128, H * H], BF16)
    ebc_in = din("c_ebc", [128, H * 128], BF16)
    cst_in = din("c_cst", [128, 16], F32)
    im_in = din("c_im", [128, 2], I32)
    info_in = din("coreinfo", [128, 2], I32)

    out_own = nc.dram_tensor("out_own", [OWN, DM], F32, kind="ExternalOutput").ap()
    scr_xp = nc.dram_tensor("scr_xp", [OWN, DM], F32, kind="Internal").ap()
    scr_xh2 = nc.dram_tensor("scr_xh2", [OWN, DM], BF16, kind="Internal").ap()
    scr_w = {p: nc.dram_tensor("scr_w" + p, [DM, DM], BF16, kind="Internal").ap()
             for p in ("q", "k", "v", "o")}
    scr_fw = {"g": nc.dram_tensor("scr_fg", [MLP, DM], BF16, kind="Internal").ap(),
              "u": nc.dram_tensor("scr_fu", [MLP, DM], BF16, kind="Internal").ap(),
              "d": nc.dram_tensor("scr_fd", [DM, MLP], BF16, kind="Internal").ap()}
    scr_int = nc.dram_tensor("scr_int", [MLP, FTOK], BF16, kind="Internal").ap()
    scrT = {"g": nc.dram_tensor("scrT_g", [32, 128, 8192], BF16, kind="Internal").ap(),
            "u": nc.dram_tensor("scrT_u", [32, 128, 8192], BF16, kind="Internal").ap(),
            "d": nc.dram_tensor("scrT_d", [MLP, DM], BF16, kind="Internal").ap()}

    SBT = 200 * KB
    SB = nc.alloc_sbuf_tensor("arena", [128, SBT], U8)

    def V(off, nelem, dtype, parts=128):
        size = {F32: 4, BF16: 2, I32: 4, U8: 1}[dtype]
        ap = SB[:, off:off + nelem * size].bitcast(dtype)
        if parts != 128:
            ap = ap[0:parts, :]
        return ap

    OFF = {}
    top = [0]

    def region(name, nbytes):
        off = (top[0] + 63) // 64 * 64
        OFF[name] = off
        top[0] = off + nbytes
        assert top[0] <= SBT, f"SBUF overflow at {name}: {top[0]}"
        return off

    region("const", 16 * KB)
    region("xroll", 16 * KB)
    region("xht", 8 * KB)
    region("xhtT", 8 * KB)
    region("qf", 8 * KB)
    region("kf", 8 * KB)
    region("ktok", 8 * KB)
    region("vtok", 8 * KB)
    region("sig", 8 * KB)
    region("on", 8 * KB)
    region("ogb", 8 * KB)
    region("s", 8 * KB)
    region("sb", 4 * KB)
    region("dt", 4 * KB)
    region("dqb", 4 * KB)
    region("cs", 4 * KB)
    region("wa", 16 * KB)
    region("scr", 12 * KB)
    region("stat", 2 * KB)
    region("dq", 23 * KB)
    region("dqout", 8 * KB)
    assert OFF["xhtT"] == OFF["xht"] + 8 * KB

    FO = {}
    FO["xf2"] = OFF["qf"]
    FO["ytok"] = OFF["ktok"]
    FO["gw"] = OFF["xroll"]
    FO["uw"] = OFF["ogb"]
    FO["dw"] = OFF["wa"]
    FO["gel"] = OFF["wa"] + 8 * KB
    FO["up"] = OFF["wa"] + 10 * KB
    FO["inc"] = OFF["wa"] + 12 * KB
    FO["yfm2"] = OFF["scr"]
    FO["xps"] = OFF["dq"]
    FO["intr"] = OFF["dqout"]
    FO["sqd"] = OFF["dqout"] + 4 * KB
    assert OFF["kf"] == OFF["qf"] + 8 * KB and OFF["ktok"] == OFF["kf"] + 8 * KB \
        and OFF["vtok"] == OFF["ktok"] + 8 * KB
    assert OFF["on"] == OFF["sig"] + 8 * KB and OFF["sig"] == OFF["vtok"] + 8 * KB
    assert OFF["xht"] == OFF["xroll"] + 16 * KB
    assert OFF["s"] == OFF["ogb"] + 8 * KB and OFF["cs"] == OFF["dqb"] + 4 * KB

    CO = {}
    cc = [OFF["const"]]

    def cdef(name, nbytes):
        o = (cc[0] + 63) // 64 * 64
        CO[name] = o
        cc[0] = o + nbytes
        assert cc[0] <= OFF["const"] + 16 * KB
    cdef("idb", 256); cdef("idf", 512)
    cdef("ind", H * H * 2); cdef("ebc", H * 128 * 2)
    cdef("dk", H * 4); cdef("nw", H * 4)
    cdef("cst", 64); cdef("im", 8)
    cdef("wg1", H * 128 * 2); cdef("wg2", H * 128 * 2)
    cdef("info", 8)

    def cval(name, nelem, dtype, parts=128):
        return V(CO[name], nelem, dtype, parts)

    for k, v in enumerate(CON["CVALS"]):
        nc.const_aps.register(F32, float(v), cval("cst", 16, F32)[:, k:k + 1])

    sch = Sch(nc)
    E = sch.emit

    def DMA(out_ap, in_ap, **kw):
        return lambda e: e.dma_start(out=out_ap, in_=in_ap, **kw)

    PB = [nc.alloc_psum_tensor(f"pb{i}", [128, 512], F32) for i in range(8)]

    def pbf(i, w=512):
        return PB[i][:, 0:w]

    def pbb(i, w=1024):
        return PB[i][:].bitcast(BF16)[:, 0:w]

    XROLL = lambda par: V(OFF["xroll"] + par * 8 * KB, 2048, F32)
    XHT = V(OFF["xht"], 2 * 2048, BF16)
    XHTT = V(OFF["xhtT"], H * TILE, BF16)
    QF = V(OFF["qf"], H * TILE, BF16)
    KF = V(OFF["kf"], H * TILE, BF16)
    KTOK = V(OFF["ktok"], 2 * 2048, BF16)
    VTOK = V(OFF["vtok"], 2 * 2048, BF16)
    SIG = V(OFF["sig"], H * TILE, BF16)
    ON = V(OFF["on"], H * TILE, BF16)
    OGB = V(OFF["ogb"], H * TILE, BF16)
    SFULL = V(OFF["s"], H * 128, F32)
    SBF = V(OFF["sb"], H * 128, BF16)
    DTT = V(OFF["dt"], H * 128, BF16)
    DQBC = V(OFF["dqb"], H * 128, BF16)
    ROT = V(OFF["xhtT"], H * TILE, BF16)
    XPP = V(OFF["xht"], 2 * 2048, F32)
    XPSL = lambda par: V(OFF["qf"] + par * 4 * KB, 1024, F32)
    XPSL = lambda par: V(OFF["qf"] + par * 8 * KB, 2048, F32)
    XH2 = lambda par: V(OFF["ogb"] + par * 4 * KB, 2048, BF16)

    def cs_view(which, parity):
        return V(OFF["cs"] + (which * 2 + parity) * 512, TILE, BF16)

    def wa_view(slot):
        return V(OFF["wa"] + slot * 4 * KB, 16 * 128, BF16).rearrange(
            "p (a c) -> p a c", c=128)

    XFM = lambda par: V(OFF["scr"] + par * KB, TILE, F32)
    G1B = V(OFF["scr"] + 2 * KB, TILE, BF16)
    QDEC = lambda par: V(OFF["scr"] + 2 * KB + 512 + par * 256, 128, BF16)
    OG2 = lambda par: V(OFF["scr"] + 3 * KB + par * 256, 128, BF16)
    KDEC = lambda par: V(OFF["scr"] + 3 * KB + 512 + par * 256, 128, BF16)
    SCM = lambda par: V(OFF["scr"] + 4 * KB + par * 256, 128, BF16)
    RMSF = V(OFF["scr"] + 4 * KB + 512, TILE, F32, parts=16)
    RMSB = V(OFF["scr"] + 5 * KB + 512, TILE, BF16, parts=16)
    SQD = V(OFF["scr"] + 6 * KB, 2048, BF16)

    def stat_col(col, parity):
        o = OFF["stat"] if True else 0
        return V(o, 512, F32)[:, parity * 16 + col: parity * 16 + col + 1]

    ID_B = cval("idb", 128, BF16)
    ID_F = cval("idf", 128, F32)

    ln_war = {}

    def ln_norm(dst_ap, src_ap, parity, eps, t_src, extra=(), sqd=None, statf=None):
        sq = sqd if sqd is not None else SQD
        sc = statf if statf is not None else stat_col
        su = sc(0, parity); m = sc(1, parity); ssq = sc(2, parity)
        var = sc(3, parity); rstd = sc(4, parity); mm_ = sc(5, parity)
        war = ln_war.get(parity)
        t1 = E("vector", lambda e: e.tensor_reduce(su, src_ap, axis=AX.X, op=ALU.add),
               waits=(t_src, war) + tuple(extra), inc="tv")
        t2 = E("scalar", lambda e: e.activation(sq, src_ap, AF.Square, accum_out=ssq),
               waits=(t_src, war, t1) + tuple(extra), inc="ts")
        t3 = E("vector", lambda e: e.tensor_scalar(m, su, 1.0 / DM, None, ALU.mult),
               waits=(t1,), inc="tv")
        t4 = E("vector", lambda e: e.tensor_scalar(var, ssq, 1.0 / DM, None, ALU.mult),
               waits=(t2,), inc="tv")
        t5 = E("vector", lambda e: e.tensor_mul(mm_, m, m), waits=(t3,), inc="tv")
        t6 = E("vector", lambda e: e.tensor_sub(var, var, mm_), waits=(t4, t5), inc="tv")
        t7 = E("scalar", lambda e: e.activation(rstd, var, AF.Sqrt, bias=eps),
               waits=(t6,), inc="ts")
        t8 = E("vector", lambda e: e.reciprocal(rstd, rstd), waits=(t7,), inc="tv")
        t9 = E("vector", lambda e: e.tensor_scalar(dst_ap, src_ap, m, rstd,
                                                   ALU.subtract, ALU.mult),
               waits=(t8,), inc="tv")
        ln_war[parity] = t9
        return t9

    pb_war = [None] * 8

    def transp(src_fn, dst3, nblk, bank, dtype, waits, evac_eng="scalar"):
        ident = ID_B if dtype == BF16 else ID_F
        pb = pbb(bank, nblk * 128) if dtype == BF16 else pbf(bank, nblk * 128)
        tok = None
        for i in range(nblk):
            w = tuple(waits) + ((pb_war[bank],) if pb_war[bank] else ()) if i == 0 else ()
            tok = E("tensor",
                    (lambda ap_in, ap_out: lambda e: e.transpose(ap_out, ap_in, ident))(
                        src_fn(i), pb[:, i * 128:(i + 1) * 128]),
                    waits=w, inc="tt" if i == nblk - 1 else None)
        src3 = pb.rearrange("p (a c) -> p a c", c=128)
        if evac_eng == "scalar":
            t_ev = E("scalar", lambda e: e.activation(dst3, src3, AF.Copy),
                     waits=(tok,), inc="ts")
        else:
            t_ev = E("vector", lambda e: e.tensor_copy(dst3, src3),
                     waits=(tok,), inc="tv")
        pb_war[bank] = t_ev
        return t_ev

    E("sync", DMA(cval("info", 2, I32), info_in[:]), inc="qs", n=16)
    QS_FIRST = 16
    for outv, inv in [
        (cval("idb", 128, BF16), idb_in[:]),
        (cval("idf", 128, F32), idf_in[:]),
        (cval("ind", H * H, BF16), ind_in[:]),
        (cval("dk", H, F32), dk_in[0:128, :]),
        (cval("nw", H, F32), nw_in[:]),
        (cval("cst", 16, F32), cst_in[:]),
        (cval("im", 2, I32), im_in[:]),
        (V(OFF["dt"], H * 128, BF16).rearrange("p (h c) -> p h c", h=H),
         dt_in.rearrange("(h p) c -> p h c", p=128)),
        (V(OFF["dqb"], H * 128, BF16).rearrange("p (h c) -> p h c", h=H),
         dqb_in.rearrange("(h p) c -> p h c", p=128)),

    ]:
        E("sync", DMA(outv, inv), inc="qs", n=16)
    E("sync", DMA(cval("ebc", H * 128, BF16), ebc_in[:]), inc="qs", n=16)
    t_const = sch.now("qs")
    t_wg1 = E("gpsimd", DMA(cval("wg1", H * 128, BF16).rearrange("p (a c) -> p a c", a=H),
                            wg1_in.rearrange("(a p) c -> p a c", p=128)), inc="qg", n=16)
    t_wg2 = E("gpsimd", DMA(cval("wg2", H * 128, BF16), wg2_in[:]), inc="qg", n=16)

    wc0 = V(OFF["dq"], 2048, BF16)
    wc1 = V(OFF["dq"] + 4 * KB, 2048, BF16)
    t_wcast = {}
    prev = [None, None]
    for p in ("k", "v", "q", "o"):
        lastst = None
        for hc in range(16):
            bv = wc0 if hc % 2 == 0 else wc1
            war = prev[hc % 2]
            tl = E("gpsimd", DMA(bv, w_in[p][hc * 128:(hc + 1) * 128, :]),
                   waits=(war,), inc="qg", n=16)
            ts_ = E("gpsimd", DMA(scr_w[p][hc * 128:(hc + 1) * 128, :], bv),
                    waits=(tl,), inc="qg", n=16)
            prev[hc % 2] = ts_
            lastst = ts_
        t_wcast[p] = lastst

    dqo = OFF["dq"]
    DQ_BY = lambda par: V(dqo + par * KB, 1024, U8)
    DQ_I1 = V(dqo + 2 * KB, 1024, U8)
    DQ_I2 = V(dqo + 3 * KB, 1024, U8)
    DQ_CF = V(dqo + 4 * KB, 1024, F32)
    DQ_SGN = V(dqo + 8 * KB, 1024, F32)
    DQ_JF = V(dqo + 12 * KB, 1024, F32)
    DQ_R1 = V(dqo + 16 * KB, 1024, BF16)
    DQ_R2 = V(dqo + 18 * KB, 1024, BF16)
    DQ_SC = lambda par: V(dqo + 20 * KB + par * 512, 128, F32)
    DQ_OUT = lambda par: V(OFF["dqout"] + par * 4 * KB, 2048, BF16)
    dqs = {"war": None, "outwar": [None, None], "ldr": [t_wcast["o"], t_wcast["o"]],
           "scwar": [None, None], "pend": None}
    dq_done_tokens = []

    def dq_core(t_b, t_s, par, dst_ap, bpar):
        nco = 1024
        im = cval("im", 2, I32).bitcast(U8)
        BY = DQ_BY(bpar)
        t_lo = E("vector", lambda e: e.tensor_scalar(DQ_I1, BY, im[:, 0:1], None,
                                                     ALU.bitwise_and),
                 waits=(t_b, t_const, dqs["war"]), inc="tv")
        t_hi = E("vector", lambda e: e.tensor_scalar(DQ_I2, BY, 4, None,
                                                     ALU.logical_shift_right),
                 waits=(t_b, dqs["war"]), inc="tv")
        outs = []
        for plane, t_pl in ((0, t_lo), (1, t_hi)):
            PLI = DQ_I1 if plane == 0 else DQ_I2
            t_cf = E("vector", (lambda PLI: lambda e: e.tensor_copy(DQ_CF, PLI))(PLI),
                     waits=(t_pl, dqs["war"]), inc="tv")
            t_sg = E("scalar", lambda e: e.activation(DQ_SGN, DQ_CF, AF.Sign, bias=-7.5),
                     waits=(t_cf,), inc="ts")
            t_j1 = E("vector", lambda e: e.scalar_tensor_tensor(DQ_JF, DQ_SGN, -4.0,
                                                                DQ_CF, ALU.mult, ALU.add),
                     waits=(t_sg,), inc="tv")
            t_j2 = E("vector", lambda e: e.tensor_scalar(DQ_JF, DQ_JF, 4.0, None,
                                                         ALU.subtract),
                     waits=(t_j1,), inc="tv")
            t_r1 = E("scalar", lambda e: e.activation(DQ_R1, DQ_JF, AF.Relu,
                                                      scale=0.5, bias=-2.0),
                     waits=(t_j2,), inc="ts")
            t_r2 = E("scalar", lambda e: e.activation(DQ_R2, DQ_JF, AF.Relu, bias=-6.0),
                     waits=(t_j2,), inc="ts")
            t_m1 = E("vector", lambda e: e.scalar_tensor_tensor(DQ_JF, DQ_JF, 0.5,
                                                                DQ_R1, ALU.mult, ALU.add),
                     waits=(t_r1,), inc="tv")
            t_m2 = E("gpsimd", lambda e: e.tensor_tensor(DQ_JF, DQ_JF, DQ_R2, ALU.add),
                     waits=(t_m1, t_r2), inc="tp")
            t_v = E("vector", lambda e: e.scalar_tensor_tensor(DQ_CF, DQ_JF, -1.0,
                                                               DQ_SGN, ALU.mult, ALU.mult),
                    waits=(t_m2,), inc="tv")
            dqs["war"] = t_v
            ov3 = DQ_OUT(par).rearrange("p (a two) -> p two a", two=2)[:, plane, :] \
                             .rearrange("p (a b) -> p a b", b=16)
            cf3 = DQ_CF.rearrange("p (a b) -> p a b", b=16)
            scb = DQ_SC(par)[:, 0:64].unsqueeze(2).broadcast_to([128, 64, 16])
            t_o = E("vector", (lambda ov3, cf3, scb: lambda e: e.tensor_tensor(
                ov3, cf3, scb, ALU.mult))(ov3, cf3, scb),
                waits=(t_v, t_s, dqs["outwar"][par]), inc="tv")
            dqs["scwar"][par] = t_o
            outs.append(t_o)
        t_w = E("sync", DMA(dst_ap, DQ_OUT(par)), waits=tuple(outs), inc="qs", n=16)
        dqs["outwar"][par] = t_w
        dqs["ldr"][bpar] = t_hi
        dq_done_tokens.append(t_w)

    dq_jobs = []
    for mat in ("g", "u"):
        for r0 in range(0, MLP, 128):
            dq_jobs.append((mat, r0, 0))
    for r0 in range(0, DM, 128):
        for cb in range(0, MLP // 2, 1024):
            dq_jobs.append(("d", r0, cb))
    dq_i = [0]

    scr_fence = nc.dram_tensor("scr_fence", [128, 64], U8, kind="Internal").ap()

    def dq_load(j):
        mat, r0, cb = dq_jobs[j]
        bpar = j % 2
        par = j % 2
        if mat == "d":
            src_ap = u8_in["d"][r0:r0 + 128, cb:cb + 1024]
            sc_ap = sc_in["d"][r0:r0 + 128, cb // 16: cb // 16 + 64]
        else:
            src_ap = u8_in[mat][r0:r0 + 128, :]
            sc_ap = sc_in[mat][r0:r0 + 128, :]
        t_b = E("gpsimd", DMA(DQ_BY(bpar), src_ap), waits=(dqs["ldr"][bpar],),
                inc="qg", n=16)
        t_f = E("gpsimd", DMA(scr_fence[:], DQ_BY(bpar)[:, 0:64]), waits=(),
                inc="qg", n=16)
        t_s = E("gpsimd", DMA(DQ_SC(par)[:, 0:64], sc_ap),
                waits=(dqs["scwar"][par],), inc="qg", n=16)
        return (t_f, t_s, par, bpar)

    def dq_drain(k):
        while k > 0 and dq_i[0] < len(dq_jobs):
            j = dq_i[0]
            dq_i[0] += 1
            k -= 1
            mat, r0, cb = dq_jobs[j]
            if dqs["pend"] is None:
                dqs["pend"] = dq_load(j)
            t_f, t_s, par, bpar = dqs["pend"]
            dqs["pend"] = dq_load(dq_i[0]) if dq_i[0] < len(dq_jobs) else None
            if mat == "d":
                dst = scr_fw["d"][r0:r0 + 128, cb * 2: cb * 2 + 2048]
            else:
                dst = scr_fw[mat][r0:r0 + 128, :]
            dq_core(t_f, t_s, par, dst, bpar)
            continue

    st = {
        "xroll": [None, None], "cs": [None, None], "wa": [None] * 4,
        "s": None, "sbf": None, "rotwar": None, "xhtwar": None,
        "qd": [None, None], "kd": [None, None], "og2": [None, None],
        "scm": [None, None], "xfm": [None, None], "xh2": [None, None],
        "xpslwar": [None, None], "sig_war": None, "ogb_war": (),
        "rms_war": None, "xpsl_store": [None, None],
    }
    t0 = E("vector", lambda e: e.memset(SFULL, 0.0), inc="tv")
    st["s"] = t0
    st["sbf"] = E("vector", lambda e: e.memset(SBF, 0.0), waits=(t0,), inc="tv")

    def a_tile(idx, src_x, row0, cosap, sinap, pos0, own):
        par_cs = idx % 2
        t_cc = E("sync", DMA(cs_view(0, par_cs), cosap[:, pos0:pos0 + TILE]),
                 waits=(st["cs"][par_cs],), inc="qs", n=16)
        t_ss = E("sync", DMA(cs_view(1, par_cs), sinap[:, pos0:pos0 + TILE]),
                 waits=(st["cs"][par_cs],), inc="qs", n=16)
        t_xh = []
        for g in range(GSL):
            par = g % 2
            t_x = E("sync", DMA(XROLL(par), src_x[row0 + g * 128: row0 + (g + 1) * 128, :]),
                    waits=(st["xroll"][par],), inc="qs", n=16)
            tk = ln_norm(XHT[:, g * 2048:(g + 1) * 2048], XROLL(par), par, EPS_LN, t_x,
                         extra=(t_const, st["xhtwar"]))
            st["xroll"][par] = tk
            t_xh.append(tk)
        if CFG["astop"] == "ln":
            return
        t_tT = None
        for g in range(GSL):
            for half in range(2):
                blocks = (lambda g, half: lambda i: XHT[:, g * 2048 + (half * 8 + i) * 128:
                                                       g * 2048 + (half * 8 + i) * 128 + 128])(g, half)
                dst3 = XHTT.rearrange("p (a c) -> p a c", c=TILE)[
                    :, half * 8:(half + 1) * 8, g * 128:(g + 1) * 128]
                t_tT = transp(blocks, dst3, 8, 2, BF16, (t_xh[g],))
        if CFG["astop"] == "transp":
            return
        t_vt = None
        t_kf_src = None
        t_qf_src = None
        projs = ["v", "k", "q"] if own else ["v", "k"]
        for p in projs:
            ev_last = None
            for dc in range(16):
                slot = dc % 2 if p != "q" else 2 + dc % 2
                wv = wa_view(slot)
                t_w = E("sync", DMA(wv, scr_w[p].rearrange("(a p) c -> p a c", p=128)[
                    :, :, dc * 128:(dc + 1) * 128]),
                        waits=(st["wa"][slot], t_wcast[p]), inc="qs", n=16)
                bank = dc % 2
                mm_last = None
                for hc in range(16):
                    w = (t_w, t_tT, pb_war[bank]) if hc == 0 else ()
                    mm_last = E("tensor",
                                (lambda wv, hc, bank, s_, p_: lambda e: e.matmul(
                                    pbf(bank, TILE), wv[:, hc, :],
                                    XHTT[:, hc * TILE:(hc + 1) * TILE],
                                    start=s_, stop=p_))(wv, hc, bank, hc == 0, hc == 15),
                                waits=w, inc="tt" if hc == 15 else None)
                st["wa"][slot] = mm_last
                if p == "v":
                    dst = OGB[:, dc * TILE:(dc + 1) * TILE]
                    fn = (lambda d, b: lambda e: e.activation(d, pbf(b, TILE), AF.Copy))(dst, bank)
                elif p == "k":
                    dst = ON[:, dc * TILE:(dc + 1) * TILE]
                    fn = (lambda d, b: lambda e: e.activation(d, pbf(b, TILE), AF.Silu))(dst, bank)
                else:
                    dst = OGB[:, dc * TILE:(dc + 1) * TILE]
                    fn = (lambda d, b: lambda e: e.activation(d, pbf(b, TILE), AF.Silu))(dst, bank)
                wts = (mm_last,)
                if p == "q" and dc == 0:
                    wts = (mm_last, t_vt)
                if p == "v" and dc == 0:
                    wts = (mm_last,) + tuple(st["ogb_war"])
                ev_last = E("scalar", fn, waits=wts, inc="ts")
                pb_war[bank] = ev_last
            if p == "v":
                for dc in range(16):
                    blocks = (lambda dc: lambda i: OGB[:, dc * TILE + i * 128:
                                                      dc * TILE + i * 128 + 128])(dc)
                    dst3 = VTOK.rearrange("p (g c) -> p g c", g=GSL)[:, :, dc * 128:(dc + 1) * 128]
                    t_vt = transp(blocks, dst3, GSL, 3, BF16, (ev_last,), evac_eng="vector")
            elif p == "k":
                t_kf_src = ev_last
            else:
                t_qf_src = ev_last
        if CFG["astop"] == "proj":
            return
        t_sig = None
        if own:
            mm_last = None
            for hc in range(16):
                w = (t_wg1, t_tT, pb_war[4]) if hc == 0 else ()
                mm_last = E("tensor", (lambda hc, s_, p_: lambda e: e.matmul(
                    pbf(4, TILE),
                    cval("wg1", H * 128, BF16).rearrange("p (a c) -> p a c", a=H)[:, hc, :],
                    XHTT[:, hc * TILE:(hc + 1) * TILE], start=s_, stop=p_))(hc, hc == 0, hc == 15),
                    waits=w, inc="tt" if hc == 15 else None)
            t_g1 = E("scalar", lambda e: e.activation(G1B, pbf(4, TILE), AF.Copy),
                     waits=(mm_last,), inc="ts")
            pb_war[4] = t_g1
            last_sig = None
            for dc in range(16):
                bank = 5 + dc % 2
                tkm = E("tensor", (lambda dc, b: lambda e: e.matmul(
                    pbf(b, TILE), cval("wg2", H * 128, BF16)[:, dc * 128:(dc + 1) * 128],
                    G1B, start=True, stop=True))(dc, bank),
                    waits=(t_g1, t_wg2, pb_war[bank]), inc="tt")
                last_sig = E("scalar", (lambda dc, b: lambda e: e.activation(
                    SIG[:, dc * TILE:(dc + 1) * TILE], pbf(b, TILE), AF.Sigmoid))(dc, bank),
                    waits=(tkm, st["sig_war"] if dc == 0 else None), inc="ts")
                pb_war[bank] = last_sig
            t_sig = last_sig

        if CFG["astop"] == "gate":
            return
        def rot(srcb, dstb, t_src):
            tmp3 = ROT.rearrange("p (a c) -> p a c", c=TILE)
            src3 = srcb.rearrange("p (a c) -> p a c", c=TILE)
            dst3 = dstb.rearrange("p (a c) -> p a c", c=TILE)
            cb = cs_view(0, par_cs).unsqueeze(1).broadcast_to([128, H, TILE])
            sb2 = cs_view(1, par_cs).unsqueeze(1).broadcast_to([128, H, TILE])
            w0 = (t_src, t_cc, t_ss, st["rotwar"], t_tT, sch.now("tt"),
                  st["xpsl_store"][0], st["xpsl_store"][1])
            a = E("vector", lambda e: e.tensor_copy(ROT[0:64, :], srcb[64:128, :]),
                  waits=w0, inc="tv")
            b = E("vector", lambda e: e.tensor_copy(ROT[64:128, :], srcb[0:64, :]),
                  waits=(a,), inc="tv")
            c2 = E("vector", lambda e: e.tensor_tensor(tmp3, tmp3, sb2, ALU.mult),
                   waits=(b,), inc="tv")
            d_ = E("vector", lambda e: e.tensor_tensor(dst3, src3, cb, ALU.mult),
                   waits=(c2,), inc="tv")
            f_ = E("vector", lambda e: e.tensor_tensor(dstb[:], dstb[:], ROT[:], ALU.add),
                   waits=(d_,), inc="tv")
            st["rotwar"] = f_
            return f_
        t_kf = rot(ON, KF, t_kf_src)
        t_qf = rot(OGB, QF, t_qf_src) if own else None
        st["xhtwar"] = st["rotwar"]
        st["cs"][par_cs] = st["rotwar"]
        if CFG["astop"] == "rot":
            return
        t_kt = None
        for dc in range(16):
            blocks = (lambda dc: lambda i: KF[:, dc * TILE + i * 128:
                                              dc * TILE + i * 128 + 128])(dc)
            dst3 = KTOK.rearrange("p (g c) -> p g c", g=GSL)[:, :, dc * 128:(dc + 1) * 128]
            t_kt = transp(blocks, dst3, GSL, 3, BF16, (t_kf,), evac_eng="vector")

        if CFG["astop"] == "ktr":
            return
        grms_toks = []
        for c in range(GSL):
            for h in range(H):
                par = (c * H + h) % 2
                t_kd = E("vector", (lambda c, h, par: lambda e: e.tensor_scalar(
                    KDEC(par), KTOK[:, c * 2048 + h * 128: c * 2048 + (h + 1) * 128],
                    cval("dk", H, F32)[:, h:h + 1], None, ALU.mult))(c, h, par),
                    waits=(t_kt, st["kd"][par]), inc="tv")
                if own:
                    bs = 2 + par * 2
                    bo = 3 + par * 2
                    t_qd = E("vector", (lambda c, h, par: lambda e: e.tensor_tensor(
                        QDEC(par), QF[:, h * TILE + c * 128: h * TILE + (c + 1) * 128],
                        DQBC[:, h * 128:(h + 1) * 128], ALU.mult))(c, h, par),
                        waits=(t_qf, st["qd"][par]), inc="tv")
                    t_sc = E("tensor", (lambda c, h, bs: lambda e: e.matmul(
                        pbf(bs, 128), KF[:, h * TILE + c * 128: h * TILE + (c + 1) * 128],
                        QF[:, h * TILE + c * 128: h * TILE + (c + 1) * 128],
                        start=True, stop=True))(c, h, bs),
                        waits=(t_kf, t_qf, pb_war[bs]), inc="tt")
                    t_scm = E("vector", (lambda c, h, par, bs: lambda e: e.tensor_tensor(
                        SCM(par), pbf(bs, 128), DTT[:, h * 128:(h + 1) * 128], ALU.mult))(c, h, par, bs),
                        waits=(t_sc, st["scm"][par]), inc="tv")
                    pb_war[bs] = t_scm
                    E("tensor", (lambda c, h, par, bo: lambda e: e.matmul(
                        pbf(bo, 128), VTOK[:, c * 2048 + h * 128: c * 2048 + (h + 1) * 128],
                        SCM(par), start=True, stop=False))(c, h, par, bo),
                        waits=(t_scm, t_vt, pb_war[bo]), inc=None)
                    t_o2 = E("tensor", (lambda h, par, bo: lambda e: e.matmul(
                        pbf(bo, 128), SBF[:, h * 128:(h + 1) * 128], QDEC(par),
                        start=False, stop=True))(h, par, bo),
                        waits=(t_qd, st["sbf"]), inc="tt")
                    st["qd"][par] = t_o2
                    st["scm"][par] = t_o2
                    t_og = E("vector", (lambda c, h, bo: lambda e: e.tensor_tensor(
                        OGB[:, h * TILE + c * 128: h * TILE + (c + 1) * 128],
                        pbf(bo, 128),
                        SIG[:, h * TILE + c * 128: h * TILE + (c + 1) * 128], ALU.mult))(c, h, bo),
                        waits=(t_o2, t_sig, t_qf), inc="tv")
                    pb_war[bo] = t_og
                    t_og2 = E("scalar", (lambda c, h, par: lambda e: e.activation(
                        OG2(par), OGB[:, h * TILE + c * 128: h * TILE + (c + 1) * 128],
                        AF.Square))(c, h, par),
                        waits=(t_og, st["og2"][par]), inc="ts")
                    t_st = E("tensor", (lambda c, h, par: lambda e: e.matmul(
                        pbf(6, TILE)[0:16, c * 128:(c + 1) * 128],
                        cval("ind", H * H, BF16).rearrange("p (a c) -> p a c", a=H)[:, h, :],
                        OG2(par), start=(h == 0), stop=(h == 15)))(c, h, par),
                        waits=(t_og2,) + ((pb_war[6],) if (h == 0 and c == 0) else ()),
                        inc="tt" if h == 15 else None)
                    st["og2"][par] = t_st if t_st else st["og2"][par]
                    if h == 15:
                        grms_toks.append(t_st)
                bst = (7 if par == 0 else 0) if own else (7 if par == 0 else 6)
                t_su = E("tensor", (lambda c, h, par, bst: lambda e: e.matmul(
                    pbf(bst, 128), KDEC(par),
                    VTOK[:, c * 2048 + h * 128: c * 2048 + (h + 1) * 128],
                    start=True, stop=True))(c, h, par, bst),
                    waits=(t_kd, t_vt, pb_war[bst]), inc="tt")
                st["kd"][par] = t_su
                t_s = E("vector", (lambda h, bst: lambda e: e.scalar_tensor_tensor(
                    SFULL[:, h * 128:(h + 1) * 128], SFULL[:, h * 128:(h + 1) * 128],
                    LAM_C[h], pbf(bst, 128), ALU.mult, ALU.add))(h, bst),
                    waits=(t_su, st["s"]), inc="tv")
                pb_war[bst] = t_s
                st["s"] = t_s
                if own:
                    st["sbf"] = E("vector", (lambda h: lambda e: e.tensor_copy(
                        SBF[:, h * 128:(h + 1) * 128], SFULL[:, h * 128:(h + 1) * 128]))(h),
                        waits=(t_s,), inc="tv")
        if not own:
            return
        if CFG["astop"] == "chunk":
            return

        st["sig_war"] = sch.now("tv")
        t_sq = E("scalar", lambda e: e.activation(RMSF, pbf(6, TILE)[0:16, :], AF.Sqrt,
                                                  scale=1.0 / HD, bias=EPS_G),
                 waits=tuple(grms_toks) + (st["rms_war"],), inc="ts")
        t_rc = E("vector", lambda e: e.reciprocal(RMSF, RMSF), waits=(t_sq,), inc="tv")
        t_rb = E("vector", lambda e: e.tensor_copy(RMSB, RMSF), waits=(t_rc,), inc="tv")
        st["rms_war"] = t_rb
        pb_war[6] = t_rb
        t_on_last = None
        for h in range(H):
            bank = 5 + h % 2
            t_bc = E("tensor", (lambda h, b: lambda e: e.matmul(
                pbf(b, TILE), cval("ebc", H * 128, BF16, parts=16)[:, h * 128:(h + 1) * 128],
                RMSB, start=True, stop=True))(h, bank),
                waits=(t_rb, pb_war[bank]), inc="tt")
            t_t1 = E("vector", (lambda h, b: lambda e: e.tensor_tensor(
                ON[:, h * TILE:(h + 1) * TILE], OGB[:, h * TILE:(h + 1) * TILE],
                pbf(b, TILE), ALU.mult))(h, bank),
                waits=(t_bc,), inc="tv")
            t_on_last = E("vector", (lambda h: lambda e: e.tensor_scalar(
                ON[:, h * TILE:(h + 1) * TILE], ON[:, h * TILE:(h + 1) * TILE],
                cval("nw", H, F32)[:, h:h + 1], None, ALU.mult))(h),
                waits=(t_t1,), inc="tv")
            pb_war[bank] = t_on_last
        if CFG["astop"] == "grms":
            return
        xpp_toks = []
        for dc in range(16):
            slot = dc % 2
            wv = wa_view(slot)
            t_w = E("sync", DMA(wv, scr_w["o"].rearrange("(a p) c -> p a c", p=128)[
                :, :, dc * 128:(dc + 1) * 128]),
                    waits=(st["wa"][slot], t_wcast["o"]), inc="qs", n=16)
            bank = dc % 2
            mm_last = None
            for ec in range(16):
                w = (t_w, t_on_last, pb_war[bank]) if ec == 0 else ()
                mm_last = E("tensor", (lambda wv, ec, bank, s_, p_: lambda e: e.matmul(
                    pbf(bank, TILE), wv[:, ec, :], ON[:, ec * TILE:(ec + 1) * TILE],
                    start=s_, stop=p_))(wv, ec, bank, ec == 0, ec == 15),
                    waits=w, inc="tt" if ec == 15 else None)
            st["wa"][slot] = mm_last
            parf = dc % 2
            t_ev = E("vector", (lambda b, parf: lambda e: e.tensor_copy(
                XFM(parf), pbf(b, TILE)))(bank, parf),
                waits=(mm_last, st["xfm"][parf]), inc="tv")
            pb_war[bank] = t_ev
            blocks = (lambda parf: lambda i: XFM(parf)[:, i * 128:(i + 1) * 128])(parf)
            dst3 = XPP.rearrange("p (g c) -> p g c", g=GSL)[:, :, dc * 128:(dc + 1) * 128]
            tk = transp(blocks, dst3, GSL, 4, F32, (t_ev,), evac_eng="vector")
            st["xfm"][parf] = tk
            xpp_toks.append(tk)
        if CFG["astop"] == "wo":
            return
        t_ln3 = None
        for g in range(GSL):
            par = g % 2
            xsl = XPP[:, g * 2048:(g + 1) * 2048]
            t_x = E("sync", DMA(XROLL(par), src_x[row0 + g * 128: row0 + (g + 1) * 128, :]),
                    waits=(st["xroll"][par],), inc="qs", n=16)
            t_n = ln_norm(XPSL(par), xsl, par, EPS_LN, xpp_toks[-1],
                          extra=tuple(xpp_toks) + (st["xpslwar"][par], sch.now("tt")))
            t_r = E("vector", (lambda par: lambda e: e.tensor_tensor(
                XPSL(par), XPSL(par), XROLL(par), ALU.add))(par),
                waits=(t_n, t_x), inc="tv")
            st["xroll"][par] = t_r
            t_xpst = E("sync", DMA(scr_xp[row0 + g * 128: row0 + (g + 1) * 128, :], XPSL(par)),
                       waits=(t_r,), inc="qs", n=16)
            st["xpsl_store"][par] = t_xpst
            t_ln3 = ln_norm(XH2(par), XPSL(par), par, EPS_LN, t_r,
                            extra=(st["xh2"][par],))
            st["xpslwar"][par] = t_ln3
            t_st2 = E("sync", DMA(scr_xh2[row0 + g * 128: row0 + (g + 1) * 128, :], XH2(par)),
                      waits=(t_ln3,), inc="qs", n=16)
            st["xh2"][par] = t_st2
        st["xhtwar"] = t_ln3
        st["ogb_war"] = (sch.now("tv"), st["xh2"][0], st["xh2"][1])

    for k in range(NSEG - 1 if CFG["prefix"] else 0):
        sch.begin_if(k)
        for ti in range(NT_OWN):
            row0 = k * OWN + ti * TILE
            a_tile(k * NT_OWN + ti, x_pre, row0, cos_in, sin_in, row0, own=False)
        sch.end_if()
    st["sbf"] = E("vector", lambda e: e.tensor_copy(SBF, SFULL),
                  waits=(st["s"], st["sbf"]), inc="tv")
    for ti in range(CFG["nown"]):
        a_tile(100 + ti, x_own, ti * TILE, cso_in, sso_in, ti * TILE, own=True)
        if CFG["dq"]:
            dq_drain(12)

    XF2 = V(FO["xf2"], H * FTOK, BF16)
    GW = lambda par: V(FO["gw"] + par * 16 * KB, H * NGR, BF16)
    UW = lambda par: V(FO["uw"] + par * 16 * KB, H * NGR, BF16)
    DW = lambda par: V(FO["dw"] + par * 2 * KB, 1024, BF16)
    GEL = lambda par: V(FO["gel"] + par * KB, FTOK, BF16)
    UPV = lambda par: V(FO["up"] + par * KB, FTOK, BF16)
    INC = lambda par: V(FO["inc"] + par * KB, FTOK, BF16)
    YFMP = lambda par: V(FO["yfm2"] + par * 2 * KB, FTOK, F32)
    YTOKA = V(FO["ytok"], 4 * 2048, F32)
    YTOK = lambda g: YTOKA[:, g * 2048:(g + 1) * 2048]
    XPS = lambda par: V(FO["xps"] + par * 8 * KB, 2048, F32)
    INTR = lambda par: V(FO["intr"] + par * KB, FTOK, BF16)
    SQD4 = V(FO["sqd"], 2048, BF16)

    fs = {"xf2": None, "gw": [None, None], "uw": [None, None], "dw": [None] * 4,
          "gel": [None, None], "up": [None, None], "inc": [None, None],
          "intr": [None] * 4, "yfm": [None, None], "ytok": None,
          "xps": [None, None], "outw": [None] * 4, "gwtok": None}
    if CFG["dq"]:
        dq_drain(10 ** 9)
    t_dq_all = dq_done_tokens[-1] if dq_done_tokens else None
    t_a_done = sch.now("qs")
    t_a_bar = (t_a_done, st["s"], st["sbf"], sch.now("tt"), sch.now("tv"), sch.now("ts"))
    tp_war = [None, None, None, None]
    GWB0 = lambda sl: V(FO["gw"] + sl * 8 * KB, 2048, BF16)
    tp_jobs = []
    for mat, src_t, dst_t in (("g", scr_fw["g"], scrT["g"]),
                              ("u", scr_fw["u"], scrT["u"])):
        for hc in range(16):
            for nb in range(8):
                tp_jobs.append((src_t[nb * 2048:(nb + 1) * 2048,
                                      hc * 128:(hc + 1) * 128],
                                [(dst_t[nb * 4 + q][:, hc * 512:(hc + 1) * 512],
                                  (q * 512, (q + 1) * 512)) for q in range(4)]))
    for nt in range(128):
        tp_jobs.append((scr_fw["d"][:, nt * 128:(nt + 1) * 128],
                        [(scrT["d"][nt * 128:(nt + 1) * 128, :], (0, 2048))]))
    tp_pend = None
    for k, (sap, dap) in enumerate(tp_jobs):
        sl = k % 4
        tl = E("scalar", DMA(GWB0(sl), sap, transpose=True),
               waits=(tp_war[sl] if tp_war[sl] else t_a_bar[0], t_dq_all)
               + (t_a_bar if k < 4 else ()),
               inc="qa", n=16)
        if tp_pend is not None:
            ptl, pdl, psl = tp_pend
            for dst_ap, (c0, c1) in pdl:
                ts2 = E("sync", DMA(dst_ap, GWB0(psl)[:, c0:c1]),
                        waits=(ptl, tl), inc="qs", n=16)
            tp_war[psl] = ts2
        tp_pend = (tl, dap, sl)
    ptl, pdl, psl = tp_pend
    tfin = E("scalar", DMA(scr_fence[:], GWB0(psl)[:, 0:32].bitcast(U8)),
             waits=(), inc="qa", n=16)
    for dst_ap, (c0, c1) in pdl:
        ts2 = E("sync", DMA(dst_ap, GWB0(psl)[:, c0:c1]),
                waits=(ptl, tfin), inc="qs", n=16)
    tp_war[psl] = ts2
    E("sync", DMA(scr_fence[:], GWB0(psl)[:, 0:32].bitcast(U8)), waits=(), inc="qs", n=16)
    t_tp_all = sch.now("qs")

    NGRP = MLP // NGR
    for p in range(CFG["npass"]):
        tok0 = p * FTOK

        def gw_load(grp):
            gpar = grp % 2
            for hhalf in range(2):
                csl = slice(hhalf * 4096, (hhalf + 1) * 4096)
                E("sync", DMA(GW(gpar)[:, csl], scrT["g"][grp][:, csl]),
                  waits=(fs["gw"][gpar], t_tp_all if p == 0 else None),
                  inc="qs", n=16)
                E("sync", DMA(UW(gpar)[:, csl], scrT["u"][grp][:, csl]),
                  waits=(fs["uw"][gpar],), inc="qs", n=16)
            return E("sync", DMA(scr_fence[:], GW(gpar)[:, 0:32].bitcast(U8)),
                     waits=(), inc="qs", n=16)

        fs[("gwt", 0)] = gw_load(0)
        for hc in range(16):
            E("sync", DMA(XF2[:, hc * FTOK:(hc + 1) * FTOK],
                          scr_xh2[tok0:tok0 + FTOK, hc * 128:(hc + 1) * 128],
                          transpose=True),
              waits=(fs["xf2"],) + (t_a_bar if (p == 0 and hc == 0) else ()),
              inc="qs", n=16)
        t_xf2 = E("sync", DMA(scr_fence[:], XF2[:, 0:32].bitcast(U8)),
                  waits=(), inc="qs", n=16)
        ytok_cols = {}
        if CFG["fstop"] == "xf2":
            fs["xf2"] = sch.now("qs")
            continue
        rounds_eff = ROUNDS[:1] if CFG["fstop"] in ("gate", "r0") else ROUNDS
        for r, hcs in enumerate(rounds_eff):
            nbank = {hc: i for i, hc in enumerate(hcs)}
            last_dw_mm = None
            pend = None
            intr_next = [0]
            intr_tok = {}

            def intr_prefetch(upto):
                while intr_next[0] <= min(upto, NNC - 1):
                    i2 = intr_next[0]
                    ip = i2 % 4
                    intr_tok[i2] = E("sync", DMA(INTR(ip), scr_int[i2 * 128:i2 * 128 + 128, :]),
                                     waits=(fs["intr"][ip],
                                            fs[("ints", p)] if i2 < 4 else None)
                                     + ((t_a_bar + (t_dq_all,)) if (p == 0 and r == 1 and i2 < 4) else ()),
                                     inc="qs", n=16)
                    intr_next[0] += 1

            dw_tok = {}
            dw_next = [0]
            ncols = len(hcs) * 128

            def dw_prefetch(upto):
                while dw_next[0] <= min(upto, NNC - 1):
                    i2 = dw_next[0]
                    dw_tok[i2] = E("sync", DMA(DW(i2 % 4)[:, 0:ncols],
                                               scrT["d"][i2 * 128:i2 * 128 + 128,
                                                         hcs[0] * 128: hcs[0] * 128 + ncols]),
                                   waits=(fs["dw"][i2 % 4],
                                          t_tp_all if (p == 0 and r == 0) else None)
                                   + (t_a_bar if (p == 0 and r == 0 and i2 < 2) else ()),
                                   inc="qs", n=16)
                    dw_next[0] += 1

            def emit_down(pd, ncix_d, fence=None):
                t_ird, int_ap, _ = pd
                dpar = ncix_d % 4
                t_dw = dw_tok[ncix_d]
                mm_last = None
                for i, hc in enumerate(hcs):
                    bank = nbank[hc]
                    w = [t_dw, t_ird, fence]
                    if ncix_d == 0:
                        w.append(pb_war[bank])
                    mm_last = E("tensor", (lambda i, bank, dpar, int_ap, s_, p_: lambda e: e.matmul(
                        pbf(bank), DW(dpar)[:, i * 128:(i + 1) * 128], int_ap,
                        start=s_, stop=p_))(i, bank, dpar, int_ap,
                                            ncix_d == 0, ncix_d == NNC - 1),
                        waits=tuple(w), inc="tt" if i == len(hcs) - 1 else None)
                fs["dw"][dpar] = mm_last
                if r == 0:
                    fs[("incrd", ncix_d % 2)] = mm_last
                else:
                    fs["intr"][ncix_d % 4] = mm_last
                return mm_last

            for ncix in range(NNC):
                nc0 = ncix * 128
                if r == 0:
                    grp = ncix // (NGR // 128)
                    within = ncix % (NGR // 128)
                    gpar = grp % 2
                    if within == 0 and grp + 1 < NGRP:
                        fs[("gwt", grp + 1)] = gw_load(grp + 1)
                    t_w = fs[("gwt", grp)]
                    gp2 = ncix % 2
                    mm_last = None
                    for hc in range(16):
                        w = (t_w, t_xf2, pb_war[6]) if hc == 0 else ()
                        mm_last = E("tensor", (lambda hc, gpar, within, s_, p_: lambda e: e.matmul(
                            pbf(6), GW(gpar)[:, hc * NGR + within * 128:
                                             hc * NGR + within * 128 + 128],
                            XF2[:, hc * FTOK:(hc + 1) * FTOK], start=s_, stop=p_))(
                            hc, gpar, within, hc == 0, hc == 15),
                            waits=w, inc="tt" if hc == 15 else None)
                    t_gel = E("scalar", (lambda gp2: lambda e: e.activation(
                        GEL(gp2), pbf(6), AF.Gelu_apprx_tanh))(gp2),
                        waits=(mm_last, fs["gel"][gp2])
                        + (t_a_bar if (p == 0 and ncix < 2) else ()), inc="ts")
                    pb_war[6] = t_gel
                    mm_last = None
                    for hc in range(16):
                        w = (t_w, t_xf2, pb_war[7]) if hc == 0 else ()
                        mm_last = E("tensor", (lambda hc, gpar, within, s_, p_: lambda e: e.matmul(
                            pbf(7), UW(gpar)[:, hc * NGR + within * 128:
                                             hc * NGR + within * 128 + 128],
                            XF2[:, hc * FTOK:(hc + 1) * FTOK], start=s_, stop=p_))(
                            hc, gpar, within, hc == 0, hc == 15),
                            waits=w, inc="tt" if hc == 15 else None)
                    if within == (NGR // 128) - 1:
                        fs["gw"][gpar] = mm_last
                        fs["uw"][gpar] = mm_last
                    t_up = E("scalar", (lambda gp2: lambda e: e.activation(
                        UPV(gp2), pbf(7), AF.Copy))(gp2),
                        waits=(mm_last, fs["up"][gp2]), inc="ts")
                    pb_war[7] = t_up
                    t_int = E("vector", (lambda gp2: lambda e: e.tensor_tensor(
                        INC(gp2), GEL(gp2), UPV(gp2), ALU.mult))(gp2),
                        waits=(t_gel, t_up, fs["inc"][gp2],
                               fs.get(("incrd", gp2))), inc="tv")
                    fs["gel"][gp2] = t_int
                    fs["up"][gp2] = t_int
                    t_is = E("sync", DMA(scr_int[nc0:nc0 + 128, :], INC(gp2)),
                             waits=(t_int,), inc="qs", n=16)
                    fs["inc"][gp2] = t_is
                    t_ird = t_int
                    int_ap = INC(gp2)
                    if ncix == NNC - 1:
                        fs[("ints", p)] = t_is
                else:
                    if CFG["fstop"] == "gate":
                        continue
                    intr_prefetch(ncix)
                    t_ird = intr_tok[ncix]
                    int_ap = INTR(ncix % 4)
                if CFG["fstop"] == "gate":
                    continue
                dw_prefetch(ncix + 2)
                if pend is not None:
                    emit_down(pend, ncix - 1, fence=dw_tok[ncix])
                    if r > 0:
                        intr_prefetch(ncix + 3)
                pend = (t_ird, int_ap, ncix)
            if CFG["fstop"] == "gate":
                continue
            t_fdw = E("sync", DMA(scr_fence[:], DW((NNC - 1) % 4)[:, 0:32].bitcast(U8)),
                      waits=(), inc="qs", n=16)
            last_dw_mm = emit_down(pend, NNC - 1, fence=t_fdw)
            if CFG["fstop"] == "gate":
                continue
            for i, hc in enumerate(hcs):
                bank = nbank[hc]
                ypar = i % 2
                t_evy = E("scalar", (lambda bank, ypar: lambda e: e.activation(
                    YFMP(ypar), pbf(bank), AF.Copy))(bank, ypar),
                    waits=(last_dw_mm, fs["yfm"][ypar]), inc="ts")
                pb_war[bank] = t_evy
                tkt = None
                for g in range(4):
                    w = (t_evy,) if g == 0 else ()
                    tkt = E("tensor", (lambda ypar, g, bank: lambda e: e.transpose(
                        pbf(bank)[:, g * 128:(g + 1) * 128],
                        YFMP(ypar)[:, g * 128:(g + 1) * 128], ID_F))(ypar, g, bank),
                        waits=w, inc="tt" if g == 3 else None)
                dst3 = YTOKA.rearrange("p (g c) -> p g c", g=4)[:, :, hc * 128:(hc + 1) * 128]
                wz = (tkt,)
                if r == 0 and i == 0:
                    wz = (tkt, fs["outw"][0], fs["outw"][1], fs["outw"][2],
                          fs["outw"][3])
                    if p == 0:
                        wz = wz + t_a_bar
                t_evt = E("vector", (lambda dst3, bank: lambda e: e.tensor_copy(
                    dst3, pbf(bank).rearrange("p (a c) -> p a c", c=128)))(dst3, bank),
                    waits=wz, inc="tv")
                pb_war[bank] = t_evt
                fs["yfm"][ypar] = t_evt
                ytok_cols[hc] = t_evt
        if CFG["fstop"] in ("xf2", "gate", "r0"):
            fs["xf2"] = sch.now("tt")
            continue
        t_cols = tuple(ytok_cols[hc] for hc in range(H))
        t_xpl = [None] * 5
        for g in range(4):
            par = g % 2
            if g == 0:
                t_xpl[0] = E("sync", DMA(XPS(0), scr_xp[tok0: tok0 + 128, :]),
                             waits=(fs["xps"][0],)
                             + ((t_a_bar + (t_dq_all,)) if p == 0 else ()),
                             inc="qs", n=16)
            if g + 1 < 4:
                t_xpl[g + 1] = E("sync", DMA(XPS((g + 1) % 2),
                                             scr_xp[tok0 + (g + 1) * 128:
                                                    tok0 + (g + 2) * 128, :]),
                                 waits=(fs["xps"][(g + 1) % 2],)
                                 + ((t_a_bar + (t_dq_all,)) if (p == 0 and g == 0) else ()),
                                 inc="qs", n=16)
            else:
                t_xpl[4] = E("sync", DMA(scr_fence[:], XPS(par)[:, 0:16].bitcast(U8)),
                             waits=(), inc="qs", n=16)
            t_n = ln_norm(YTOK(g), YTOK(g), 4 + g % 2, EPS_LN, t_cols[-1],
                          extra=t_cols, sqd=SQD4)
            t_add = E("vector", (lambda g, par: lambda e: e.tensor_tensor(
                YTOK(g), YTOK(g), XPS(par), ALU.add))(g, par),
                waits=(t_n, t_xpl[g], t_xpl[g + 1]), inc="tv")
            fs["xps"][par] = t_add
            t_o = E("sync", DMA(out_own[tok0 + g * 128: tok0 + (g + 1) * 128, :], YTOK(g)),
                    waits=(t_add,), inc="qs", n=16)
            fs["outw"][g] = t_o
        fs["xf2"] = sch.now("tt")
        fs["ytok"] = fs["outw"][3]

    if CFG["dump"]:
        dump_xp = nc.dram_tensor("dump_xp", [OWN, DM], F32, kind="ExternalOutput").ap()
        dump_xh2 = nc.dram_tensor("dump_xh2", [OWN, DM], BF16, kind="ExternalOutput").ap()
        E("gpsimd", DMA(dump_xp[:], scr_xp[:]), waits=(t_a_done,), inc="qg", n=16)
        E("gpsimd", DMA(dump_xh2[:], scr_xh2[:]), waits=(t_a_done,), inc="qg", n=16)
        dump_fg = nc.dram_tensor("dump_fg", [MLP, DM], BF16, kind="ExternalOutput").ap()
        dump_fg2 = nc.dram_tensor("dump_fg2", [MLP, DM], BF16, kind="ExternalOutput").ap()
        dump_fd = nc.dram_tensor("dump_fd", [DM, MLP], BF16, kind="ExternalOutput").ap()
        wdq = (t_dq_all, sch.now("qs"), sch.now("tv"), sch.now("ts"), sch.now("tp"),
               sch.now("qa"), sch.now("qg"))
        t_d1 = E("gpsimd", DMA(dump_fg[:], scr_fw["g"][:]), waits=wdq, inc="qg", n=16)
        t_d2 = E("gpsimd", DMA(dump_fg2[:], scr_fw["g"][:]), waits=(t_d1,), inc="qg", n=16)
        E("gpsimd", DMA(dump_fd[:], scr_fw["d"][:]), waits=wdq, inc="qg", n=16)
        E("gpsimd", lambda e: None, waits=(sch.now("qg"),), inc=None)
    t_last = sch.now("qs")
    E("sync", lambda e: None, waits=(t_last,), inc=None)

    with nc.Block() as block:
        sch.replay(block, cval("info", 2, I32, parts=1)[0:1, 0:1], QS_FIRST)
    return nc


def kernel(L0_attn_x, L0_attn_Wq, L0_attn_Wk, L0_attn_Wv, L0_attn_Wo,
           L0_attn_Wg1, L0_attn_Wg2, L0_attn_norm_weight,
           L0_ffn_gate_W_packed, L0_ffn_gate_scales, L0_ffn_up_W_packed,
           L0_ffn_up_scales, L0_ffn_down_W_packed, L0_ffn_down_scales):
    CON = host_constants()
    x = np.ascontiguousarray(np.asarray(L0_attn_x, dtype=np.float32))

    def u8(a):
        return np.ascontiguousarray(np.asarray(a, np.int32).view(np.uint8)[:, 0::4])

    common = dict(
        w_q=np.asarray(L0_attn_Wq, np.float32), w_k=np.asarray(L0_attn_Wk, np.float32),
        w_v=np.asarray(L0_attn_Wv, np.float32), w_o=np.asarray(L0_attn_Wo, np.float32),
        wg1=np.asarray(L0_attn_Wg1, np.float32), wg2=np.asarray(L0_attn_Wg2, np.float32),
        nw_pc=np.ascontiguousarray(np.asarray(L0_attn_norm_weight, np.float32)
                                   .reshape(H, 128).T),
        g_u8=u8(L0_ffn_gate_W_packed), u_u8=u8(L0_ffn_up_W_packed),
        d_u8=u8(L0_ffn_down_W_packed),
        g_sc=np.asarray(L0_ffn_gate_scales, np.float32),
        u_sc=np.asarray(L0_ffn_up_scales, np.float32),
        d_sc=np.asarray(L0_ffn_down_scales, np.float32),
        cos_fm=CON["cos_fm"], sin_fm=CON["sin_fm"],
        c_dt=CON["DT"], c_dqb=CON["dqb"], c_dk=CON["dk"],
        c_idb=CON["idb"], c_idf=CON["idf"], c_ind=CON["ind"], c_ebc=CON["ebc"],
        c_cst=CON["cst"], c_im=CON["im"])
    in_maps = []
    for c in range(8):
        b, seg = c // 4, c % 4
        m = dict(common)
        m["x_own"] = np.ascontiguousarray(x[b, seg * OWN:(seg + 1) * OWN, :])
        m["x_pre"] = np.ascontiguousarray(x[b, : (NSEG - 1) * OWN, :])
        m["cso"] = np.ascontiguousarray(CON["cos_fm"][:, seg * OWN:(seg + 1) * OWN])
        m["sso"] = np.ascontiguousarray(CON["sin_fm"][:, seg * OWN:(seg + 1) * OWN])
        m["coreinfo"] = np.tile(np.array([[seg, c]], np.int32), (128, 1))
        in_maps.append(m)
    nc = build_program()
    res = run_bass_kernel_spmd(nc, in_maps, core_ids=list(range(8)))
    out = np.zeros((B, N, DM), np.float32)
    for c in range(8):
        b, seg = c // 4, c % 4
        out[b, seg * OWN:(seg + 1) * OWN, :] = res.results[c]["out_own"]
    if CFG["dump"]:
        kernel.dumps = [dict(r) for r in res.results]
    return out
